# version 24
# baseline (speedup 1.0000x reference)
import numpy as np
import ml_dtypes
import concourse.bass as bass
import concourse.mybir as mybir
from concourse.bass_utils import run_bass_kernel_spmd

F32 = mybir.dt.float32
BF16 = mybir.dt.bfloat16
I32 = mybir.dt.int32
U32 = mybir.dt.uint32
AF = mybir.ActivationFunctionType
ALU = mybir.AluOpType
AX = mybir.AxisListType

D = 1024
NCORES = 8
SEQ = 4096

ENGINES = ("sync", "vector", "scalar", "tensor", "gpsimd")


class Prog:

    def __init__(self, nc):
        self.nc = nc
        self.ops = {e: [] for e in ENGINES}
        self.sems = {}
        self.val = {}
        for e in ENGINES:
            self.sems[e] = nc.semaphore(f"sem_{e}").__enter__()
            self.val[e] = 0
        self.dma_streams = {}
        self.last_w = {}
        self.readers = {}
        self.waited = {e: {} for e in ENGINES}
        self.guards = None

    @staticmethod
    def _name(ap):
        t = getattr(ap, "tensor", ap)
        return t.name

    @staticmethod
    def _region(ap):
        t = getattr(ap, "tensor", None)
        if t is None:
            return None
        try:
            shape = list(t.shape)
            if len(shape) != 2:
                return None
            rowlen = shape[1]
            off = ap.offset
            dims = [list(d) for d in ap.ap]
            if not dims:
                return None
            p_step, p_cnt = dims[0]
            if p_step == rowlen or shape[0] == 1 or p_cnt == 1:
                p0 = off // rowlen
                f0 = off % rowlen
                fext = 1
                for st, cnt in dims[1:]:
                    fext += (cnt - 1) * abs(st)
                return (p0, p0 + p_cnt, f0, f0 + fext)
        except Exception:
            pass
        return None

    @staticmethod
    def _overlap(a, b):
        if a is None or b is None:
            return True
        return a[0] < b[1] and b[0] < a[1] and a[2] < b[3] and b[2] < a[3]

    def _space(self, ap):
        t = getattr(ap, "tensor", None)
        n = type(t).__name__ if t is not None else ""
        if "PSum" in n:
            return "psum"
        if "SB" in n:
            return "sbuf"
        return "dram"

    def _wait(self, eng, semkey, v):
        if semkey.startswith("d_"):
            v = self.val[semkey]
        if v <= 0:
            return
        w = self.waited[eng]
        if w.get(semkey, 0) >= v:
            return
        w[semkey] = v
        sem = self.sems[semkey]
        self.ops[eng].append(lambda E, sem=sem, v=v: E.wait_ge(sem, v))

    def _entries(self, table, name):
        return table.setdefault(name, [])

    def _compact(self, lst):
        if len(lst) > 48:
            best = {}
            for semkey, v, reg, eng in lst:
                cur = best.get(semkey)
                if cur is None or v > cur[1]:
                    best[semkey] = (semkey, v, None, eng)
            lst[:] = list(best.values())

    def _pre(self, eng, reads, writes):
        for ap in reads:
            n = self._name(ap)
            reg = self._region(ap)
            full_bank = self._space(ap) == "psum"
            for semkey, v, wreg, weng in self._entries(self.last_w, n):
                creg = None if (full_bank and weng != eng) else reg
                wreg2 = None if (full_bank and weng != eng) else wreg
                if self._overlap(creg, wreg2):
                    self._wait(eng, semkey, v)
        for ap in writes:
            n = self._name(ap)
            reg = self._region(ap)
            full_bank = self._space(ap) == "psum"
            for semkey, v, wreg, weng in self._entries(self.last_w, n):
                creg = None if (full_bank and weng != eng) else reg
                wreg2 = None if (full_bank and weng != eng) else wreg
                if self._overlap(creg, wreg2):
                    self._wait(eng, semkey, v)
            for semkey, v, rreg, reng in self._entries(self.readers, n):
                creg = None if (full_bank and reng != eng) else reg
                rreg2 = None if (full_bank and reng != eng) else rreg
                if self._overlap(creg, rreg2):
                    self._wait(eng, semkey, v)

    def _post(self, semkey, newval, reads, writes, eng=None):
        for ap in reads:
            n = self._name(ap)
            lst = self._entries(self.readers, n)
            lst.append((semkey, newval, self._region(ap), eng))
            self._compact(lst)
        for ap in writes:
            n = self._name(ap)
            reg = self._region(ap)
            wl = self._entries(self.last_w, n)
            if reg is None:
                wl[:] = [(semkey, newval, None, eng)]
                self.readers[n] = []
            else:
                wl[:] = [e for e in wl if not (e[2] is not None and
                                               reg[0] <= e[2][0] and e[2][1] <= reg[1] and
                                               reg[2] <= e[2][2] and e[2][3] <= reg[3])]
                wl.append((semkey, newval, reg, eng))
                self._compact(wl)
                rl = self._entries(self.readers, n)
                rl[:] = [e for e in rl if not (e[2] is not None and
                                               reg[0] <= e[2][0] and e[2][1] <= reg[1] and
                                               reg[2] <= e[2][2] and e[2][3] <= reg[3])]

    def op(self, eng, fn, reads=(), writes=()):
        self._pre(eng, reads, writes)
        self.val[eng] += 1
        newval = self.val[eng]
        sem = self.sems[eng]

        def emit(E, fn=fn, sem=sem):
            fn(E).then_inc(sem, 1)
        self.ops[eng].append(emit)
        self._post(eng, newval, reads, writes, eng=eng)

    def dma(self, queue, out, in_, stream=None, **kw):
        semkey = "d_" + (stream if stream is not None else queue)
        if semkey not in self.sems:
            self.sems[semkey] = self.nc.semaphore(semkey).__enter__()
            self.val[semkey] = 0
            self.dma_streams[semkey] = queue
        assert self.dma_streams[semkey] == queue, (semkey, queue)
        self._wait(queue, semkey, self.val[semkey])
        self._pre(queue, [in_], [out])
        self.val[semkey] += 16
        newval = self.val[semkey]
        sem = self.sems[semkey]

        def emit(E, out=out, in_=in_, kw=kw, sem=sem):
            E.dma_start(out, in_, **kw).then_inc(sem, 16)
        self.ops[queue].append(emit)
        self._post(semkey, newval, [in_], [out], eng=queue)

    def raw(self, eng, fn):
        self.ops[eng].append(fn)

    def salloc(self, name, shape, dtype):
        g = self.nc.sbuf_tensor(name, shape, dtype)
        h = g.__enter__()
        if self.guards is not None:
            self.guards.append(g)
        return h[:]

    def palloc(self, name, shape, dtype):
        g = self.nc.psum_tensor(name, shape, dtype)
        h = g.__enter__()
        if self.guards is not None:
            self.guards.append(g)
        return h[:]

    def barrier(self):
        for eng in ENGINES:
            for semkey, v in self.val.items():
                self._wait(eng, semkey, v)

    def final_waits(self):
        for semkey, q in self.dma_streams.items():
            v = self.val[semkey]
            if v > 0:
                sem = self.sems[semkey]
                self.ops[q].append(lambda E, sem=sem, v=v: E.wait_ge(sem, v))

    def finalize(self):
        nc = self.nc
        block = nc.Block().__enter__()
        for ename in ENGINES:
            oplist = self.ops[ename]
            if not oplist:
                continue
            deco = getattr(block, ename)

            def body(E, oplist=oplist):
                for f in oplist:
                    f(E)
            deco(body)
        block.__exit__(None, None, None)

    def v_tt(self, out, a, b, alu):
        self.op("vector", lambda E: E.tensor_tensor(out, a, b, alu),
                reads=[a, b], writes=[out])

    def v_ts(self, out, a, s1, alu, s2=None, alu2=None):
        r = [a]
        if hasattr(s1, "tensor"):
            r.append(s1)
        if s2 is not None and hasattr(s2, "tensor"):
            r.append(s2)
        if s2 is None and alu2 is None:
            self.op("vector", lambda E: E.tensor_single_scalar(out, a, s1, alu),
                    reads=r, writes=[out])
        else:
            self.op("vector",
                    lambda E: E.tensor_scalar(out, a, s1, s2, alu, alu2),
                    reads=r, writes=[out])

    def v_copy(self, out, a):
        self.op("vector", lambda E: E.tensor_copy(out, a), reads=[a], writes=[out])

    def v_red(self, out, a, alu, absval=False):
        self.op("vector",
                lambda E: E.tensor_reduce(out, a, AX.X, alu,
                                          apply_absolute_value=absval),
                reads=[a], writes=[out])

    def v_stt(self, out, in0, scal, in1, alu0, alu1):
        r = [in0, in1]
        if hasattr(scal, "tensor"):
            r.append(scal)
        self.op("vector",
                lambda E: E.scalar_tensor_tensor(out, in0, scal, in1, alu0, alu1),
                reads=r, writes=[out])

    def v_scan(self, out, d0, d1, init, alu0, alu1):
        r = [d0, d1]
        if hasattr(init, "tensor"):
            r.append(init)
        self.op("vector",
                lambda E: E.tensor_tensor_scan(out, d0, d1, init, alu0, alu1),
                reads=r, writes=[out])

    def v_recip(self, out, a):
        self.op("vector", lambda E: E.reciprocal(out, a), reads=[a], writes=[out])

    def v_memset(self, out, c):
        self.op("vector", lambda E: E.memset(out, c), reads=[], writes=[out])

    def g_memset(self, out, c):
        self.op("gpsimd", lambda E: E.memset(out, c), reads=[], writes=[out])

    def g_tt(self, out, a, b, alu):
        self.op("gpsimd", lambda E: E.tensor_tensor(out, a, b, alu),
                reads=[a, b], writes=[out])

    def s_act(self, out, a, func, bias=0.0, scale=1.0, accum=None):
        r = [a]
        if hasattr(bias, "tensor"):
            r.append(bias)
        if hasattr(scale, "tensor"):
            r.append(scale)
        w = [out] + ([accum] if accum is not None else [])
        if accum is None:
            self.op("scalar",
                    lambda E: E.activation(out, a, func, bias=bias, scale=scale),
                    reads=r, writes=w)
        else:
            self.op("scalar",
                    lambda E: E.activation(out, a, func, bias=bias, scale=scale,
                                           accum_out=accum),
                    reads=r, writes=w)

    def s_copy(self, out, a):
        self.op("scalar", lambda E: E.copy(out, a), reads=[a], writes=[out])

    def mm_acc(self, out, pairs):
        n = len(pairs)
        allreads = []
        for (l, r) in pairs:
            allreads.append(l)
            allreads.append(r)
        self._pre("tensor", allreads, [out])
        for i, (l, r) in enumerate(pairs[:-1]):
            self.ops["tensor"].append(
                lambda E, out=out, l=l, r=r, st=(i == 0):
                E.matmul(out, l, r, start=st, stop=False))
        self.val["tensor"] += 1
        newval = self.val["tensor"]
        sem = self.sems["tensor"]
        l, r = pairs[-1]

        def emit(E, out=out, l=l, r=r, st=(n == 1), sem=sem):
            E.matmul(out, l, r, start=st, stop=True).then_inc(sem, 1)
        self.ops["tensor"].append(emit)
        self._post("tensor", newval, allreads, [out], eng="tensor")

    def mm(self, out, lhsT, rhs):
        self.mm_acc(out, [(lhsT, rhs)])

    def mm_part(self, out, lhsT, rhs, start, stop, final):
        self._pre("tensor", [lhsT, rhs], [out] if start else [])
        if final:
            self.val["tensor"] += 1
            newval = self.val["tensor"]
            sem = self.sems["tensor"]

            def emit(E, out=out, l=lhsT, r=rhs, start=start, stop=stop, sem=sem):
                E.matmul(out, l, r, start=start, stop=stop).then_inc(sem, 1)
            self.ops["tensor"].append(emit)
            self._post("tensor", newval, [lhsT, rhs], [out], eng="tensor")
        else:
            self.ops["tensor"].append(
                lambda E, out=out, l=lhsT, r=rhs, start=start, stop=stop:
                E.matmul(out, l, r, start=start, stop=stop))

    def transpose(self, out_ps, in_sb, ident):
        self._pre("tensor", [in_sb, ident], [out_ps])
        self.val["tensor"] += 1
        newval = self.val["tensor"]
        sem = self.sems["tensor"]

        def emit(E, out=out_ps, a=in_sb, i=ident, sem=sem):
            E.transpose(out, a, i).then_inc(sem, 1)
        self.ops["tensor"].append(emit)
        self._post("tensor", newval, [in_sb, ident], [out_ps], eng="tensor")



def const_arrays():
    c = {}
    c["zz_ident"] = np.eye(128, dtype=ml_dtypes.bfloat16)
    m = np.zeros((128, 128), np.float32)
    for c0 in (0, 64):
        for s in range(64):
            m[c0 + s, c0 + s:c0 + 64] = 1.0
    c["zz_cum64"] = m
    c["zz_cum128"] = np.fromfunction(
        lambda s, t: (s <= t).astype(np.float32), (128, 128)).astype(np.float32)
    c["zz_mask_incl64"] = np.fromfunction(
        lambda s, t: (s <= t).astype(np.float32), (64, 64)).astype(np.float32)
    c["zz_mask_strict128"] = np.fromfunction(
        lambda s, t: (s < t).astype(np.float32), (128, 128)).astype(np.float32)
    oh = np.zeros((128, 2), ml_dtypes.bfloat16)
    oh[63, 0] = 1
    oh[127, 1] = 1
    c["zz_onehot"] = oh
    c["zz_eyef"] = np.eye(128, dtype=np.float32)
    return c


class Ctx:
    pass


def setup_consts(P, nc):
    cx = Ctx()
    names = {"zz_ident": ([128, 128], BF16), "zz_cum64": ([128, 128], F32),
             "zz_cum128": ([128, 128], F32), "zz_mask_incl64": ([64, 64], F32),
             "zz_mask_strict128": ([128, 128], F32), "zz_onehot": ([128, 2], BF16),
             "zz_eyef": ([128, 128], F32)}
    for n, (shape, dt) in names.items():
        h = nc.dram_tensor(n, shape, dt, kind="ExternalInput").ap()
        sb = nc.alloc_sbuf_tensor("c_" + n[3:], shape, dt)[:]
        P.dma("sync", sb, h, stream="consts")
        setattr(cx, n[3:], sb)
    cx.cols = nc.alloc_sbuf_tensor("c_cols", [128, 8], F32)[:]
    P.v_memset(cx.cols[:, 0:1], 1e-5)
    P.v_memset(cx.cols[:, 1:2], 1e-24)
    P.v_memset(cx.cols[:, 2:3], 1.0)
    P.v_memset(cx.cols[:, 3:4], 0.5)
    P.v_memset(cx.cols[:, 4:5], -0.0625)
    P.v_memset(cx.cols[:, 5:6], 0.0)
    P.v_memset(cx.cols[:, 6:7], -0.5)
    P.v_memset(cx.cols[:, 7:8], 0.0)
    cx.ones_bf = nc.alloc_sbuf_tensor("c_onesbf", [1, 128], BF16)[:]
    P.v_memset(cx.ones_bf, 1.0)
    cx.kcol = nc.alloc_sbuf_tensor("c_kcol", [128, 1], I32)[:]
    P.v_memset(cx.kcol, 0x5F3759DF)
    return cx


def load_weight_bf16(P, nc, name, hbm_ap, rows, cols):
    nk = rows // 128
    sb = P.salloc(name, [128, nk * cols], BF16)
    src = hbm_ap.rearrange("(k p) n -> p k n", p=128)
    dst = sb.rearrange("p (k n) -> p k n", k=nk)
    P.dma("gpsimd", dst, src, stream="weights")
    return sb


def ln_stats(P, x_in, scratch, stat, eps_col):
    ss, sums, mcol, rstd, negmr = (stat[:, i:i + 1] for i in range(5))
    P.s_act(scratch, x_in, AF.Square, accum=ss)
    P.v_red(sums, x_in, ALU.add)
    P.v_ts(mcol, sums, 1.0 / D, ALU.mult)
    P.v_ts(rstd, ss, 1.0 / D, ALU.mult)
    P.v_tt(negmr, mcol, mcol, ALU.mult)
    P.v_tt(rstd, rstd, negmr, ALU.subtract)
    P.s_act(rstd, rstd, AF.Ln, bias=eps_col)
    P.s_act(rstd, rstd, AF.Exp, scale=-0.5)
    P.v_tt(negmr, mcol, rstd, ALU.mult)
    P.v_ts(negmr, negmr, -1.0, ALU.mult)
    return rstd, negmr


def transpose_1024(P, cx, src_bf, dst_ps, dst_sb, evac="vector", nblk=8):
    for k in range(nblk):
        P.transpose(dst_ps[:, k * 128:(k + 1) * 128],
                    src_bf[:, k * 128:(k + 1) * 128], cx.ident)
    if evac == "vector":
        P.v_copy(dst_sb[:, 0:nblk * 128], dst_ps[:, 0:nblk * 128])
    else:
        P.s_copy(dst_sb[:, 0:nblk * 128], dst_ps[:, 0:nblk * 128])


def proj_1024(P, out_ps_pair, xT, w_sb, nk=8, ncols=1024):
    nh = (ncols + 511) // 512
    for half in range(nh):
        w0 = half * 512
        w1 = min(ncols, w0 + 512)
        pairs = []
        for k in range(nk):
            lhsT = xT[:, k * 128:(k + 1) * 128]
            rhs = w_sb[:, k * ncols + w0: k * ncols + w1]
            pairs.append((lhsT, rhs))
        P.mm_acc(out_ps_pair[half][:, 0:w1 - w0], pairs)


def ln_prepass(P, nc, cx, x_hbm, n_tiles, tag, width=D, bufs=None,
               stats_in=None):
    A = lambda n, s, d: P.salloc(n, s, d)
    rstd = A(f"{tag}_rstd", [128, n_tiles], F32)
    negmr = A(f"{tag}_negmr", [128, n_tiles], F32)
    outer_guards = P.guards
    P.guards = []
    if stats_in is not None:
        sums, ssq = stats_in
        t1 = bufs[2][:, 0:n_tiles]
        t2 = bufs[2][:, n_tiles:2 * n_tiles]
    else:
        t1 = A(f"{tag}_t1", [128, n_tiles], F32)
        t2 = A(f"{tag}_t2", [128, n_tiles], F32)
        sums = A(f"{tag}_sums", [128, n_tiles], F32)
        ssq = A(f"{tag}_ssq", [128, n_tiles], F32)
        if bufs is None:
            xb = [A(f"{tag}_px{j}", [128, width], F32) for j in range(2)]
            scr = A(f"{tag}_pscr", [128, width], F32)
        else:
            xb = [bufs[0], bufs[1]]
            scr = bufs[2]
        x_t = x_hbm.rearrange("(n p) d -> n p d", p=128)
        for i in range(n_tiles):
            xi = xb[i % 2]
            P.dma("sync", xi, x_t[i], stream=f"pre{i % 2}")
            P.s_act(scr, xi, AF.Square, accum=ssq[:, i:i + 1])
            P.v_red(sums[:, i:i + 1], xi, ALU.add)
    eps = cx.cols[:, 0:1]
    P.v_ts(sums, sums, 1.0 / width, ALU.mult)
    P.v_ts(ssq, ssq, 1.0 / width, ALU.mult)
    P.v_tt(t1, sums, sums, ALU.mult)
    P.v_tt(ssq, ssq, t1, ALU.subtract)
    P.v_ts(ssq, ssq, eps, ALU.add)
    rsqrt_col(P, cx, rstd, ssq, t1, t2)
    P.v_tt(negmr, sums, rstd, ALU.mult)
    P.v_ts(negmr, negmr, -1.0, ALU.mult)
    P.barrier()
    for g in reversed(P.guards):
        g.__exit__(None, None, None)
    P.guards = outer_guards
    return rstd, negmr


def ln_prepass_double(P, nc, cx, x_hbm, n_tiles, tag, bufs=None,
                      stats_in=None):
    A = lambda n, s, d: P.salloc(n, s, d)
    r1 = A(f"{tag}_r1", [128, n_tiles], F32)
    f2o = A(f"{tag}_f2", [128, n_tiles], F32)
    outer_guards = P.guards
    P.guards = []
    if stats_in is not None:
        sums, ssq = stats_in
        s2 = bufs[2][:, 0:n_tiles]
        t1 = bufs[2][:, n_tiles:2 * n_tiles]
        t2 = bufs[2][:, 2 * n_tiles:3 * n_tiles]
    else:
        s2 = A(f"{tag}_s2", [128, n_tiles], F32)
        t1 = A(f"{tag}_t1", [128, n_tiles], F32)
        t2 = A(f"{tag}_t2", [128, n_tiles], F32)
        sums = A(f"{tag}_sums", [128, n_tiles], F32)
        ssq = A(f"{tag}_ssq", [128, n_tiles], F32)
        if bufs is None:
            xb = [A(f"{tag}_px{j}", [128, D], F32) for j in range(2)]
            scr = A(f"{tag}_pscr", [128, D], F32)
        else:
            xb = [bufs[0], bufs[1]]
            scr = bufs[2]
        x_t = x_hbm.rearrange("(n p) d -> n p d", p=128)
        for i in range(n_tiles):
            xi = xb[i % 2]
            P.dma("sync", xi, x_t[i], stream=f"pre{i % 2}")
            P.s_act(scr, xi, AF.Square, accum=ssq[:, i:i + 1])
            P.v_red(sums[:, i:i + 1], xi, ALU.add)
    eps = cx.cols[:, 0:1]
    epsb = eps
    P.v_ts(sums, sums, 1.0 / D, ALU.mult)
    P.v_ts(ssq, ssq, 1.0 / D, ALU.mult)
    P.v_tt(t1, sums, sums, ALU.mult)
    P.v_tt(ssq, ssq, t1, ALU.subtract)
    P.v_ts(ssq, ssq, epsb, ALU.add)
    rsqrt_col(P, cx, r1, ssq, t1, t2)
    P.v_recip(t1, ssq)
    P.v_ts(ssq, ssq, epsb, ALU.subtract)
    P.v_tt(t1, ssq, t1, ALU.mult)
    P.v_ts(t1, t1, epsb, ALU.add)
    rsqrt_col(P, cx, s2, t1, t2, ssq)
    P.v_tt(r1, r1, s2, ALU.mult)
    P.v_tt(f2o, sums, r1, ALU.mult)
    P.v_ts(f2o, f2o, -1.0, ALU.mult)
    P.barrier()
    for g in reversed(P.guards):
        g.__exit__(None, None, None)
    P.guards = outer_guards
    return r1, f2o


def emit_out_stats(P, stats_out, xo, dump, i):
    sums, ssq = stats_out
    P.s_act(dump, xo, AF.Square, accum=ssq[:, i:i + 1])
    P.v_red(sums[:, i:i + 1], xo, ALU.add)


def rsqrt_col(P, cx, out, v, s1, s2):
    W = v.shape[-1]
    P.v_ts(s1.bitcast(I32), v.bitcast(I32), 1, ALU.logical_shift_right)
    P.v_tt(out.bitcast(I32), cx.kcol.broadcast_to([128, W]), s1.bitcast(I32),
           ALU.subtract)
    for _ in range(3):
        P.v_tt(s1, out, out, ALU.mult)
        P.v_tt(s2, s1, v, ALU.mult)
        P.v_ts(s2, s2, -0.5, ALU.mult, 1.5, ALU.add)
        P.v_tt(out, out, s2, ALU.mult)


def ln_stats_nolut(P, cx, x_in, scratch, stat, eps_col, width=D):
    ss, sums, mcol, rstd, negmr, t1, t2, var = (stat[:, i:i + 1] for i in range(8))
    P.s_act(scratch[:, 0:width], x_in, AF.Square, accum=ss)
    P.v_red(sums, x_in, ALU.add)
    P.v_ts(mcol, sums, 1.0 / width, ALU.mult)
    P.v_ts(var, ss, 1.0 / width, ALU.mult)
    P.v_tt(t1, mcol, mcol, ALU.mult)
    P.v_tt(var, var, t1, ALU.subtract)
    P.v_ts(var, var, eps_col, ALU.add)
    rsqrt_col(P, cx, rstd, var, t1, t2)
    P.v_tt(negmr, mcol, rstd, ALU.mult)
    P.v_ts(negmr, negmr, -1.0, ALU.mult)
    return rstd, negmr


class QS:

    def __init__(self, P, tag, W=1024):
        A = lambda n, s, d: P.salloc(n, s, d)
        self.y = A(f"{tag}_y", [128, W], F32)
        self.g = A(f"{tag}_g", [128, W], F32)
        self.i32 = A(f"{tag}_i32", [128, W], I32)
        self.fm8 = self.i32.bitcast(F32)
        self.am = A(f"{tag}_am", [128, W // 32], F32)
        self.inv = A(f"{tag}_inv", [128, W // 32], F32)
        self.sc = A(f"{tag}_sc", [128, W // 32], F32)


def quant_chain(P, cx, q, src, dst_bf, W=1024, gt_eng="vector",
                y_eng="vector"):
    nb = W // 32
    src3 = src.rearrange("p (b k) -> p b k", k=32)
    am3 = q.am[:, 0:nb].rearrange("p (b o) -> p b o", o=1)
    P.v_red(am3, src3, ALU.max, absval=True)
    P.v_ts(q.am[:, 0:nb], q.am[:, 0:nb], 1.1754944e-38, ALU.max)
    P.v_ts(q.am[:, 0:nb].bitcast(U32), q.am[:, 0:nb].bitcast(U32),
           0x7F800000, ALU.bitwise_and)
    P.v_recip(q.inv[:, 0:nb], q.am[:, 0:nb])
    P.v_ts(q.sc[:, 0:nb], q.am[:, 0:nb], 0.125, ALU.mult)
    inv_bc = q.inv[:, 0:nb].rearrange("p (b o) -> p b o", o=1).broadcast_to([128, nb, 32])
    sc_bc = q.sc[:, 0:nb].rearrange("p (b o) -> p b o", o=1).broadcast_to([128, nb, 32])
    y3 = q.y[:, 0:W].rearrange("p (b k) -> p b k", k=32)
    if y_eng == "gpsimd":
        P.g_tt(y3, src3, inv_bc, ALU.mult)
    else:
        P.v_tt(y3, src3, inv_bc, ALU.mult)
    P.s_act(q.i32[:, 0:W], q.y[:, 0:W], AF.Identity, scale=8.0, bias=cx.cols[:, 3:4])
    P.s_act(q.fm8[:, 0:W], q.i32[:, 0:W], AF.Identity, scale=0.125, bias=cx.cols[:, 4:5])
    if gt_eng == "gpsimd":
        P.g_tt(q.g[:, 0:W], q.fm8[:, 0:W], q.y[:, 0:W], ALU.is_gt)
    else:
        P.v_tt(q.g[:, 0:W], q.fm8[:, 0:W], q.y[:, 0:W], ALU.is_gt)
    P.v_stt(q.y[:, 0:W], q.fm8[:, 0:W], 8.0, q.g[:, 0:W], ALU.mult, ALU.subtract)
    dst3 = dst_bf.rearrange("p (b k) -> p b k", k=32)
    P.v_stt(dst3, y3, 0.5, sc_bc, ALU.add, ALU.mult)



def build_pass2(P, nc, cx, x_hbm, out_hbm, n_tiles, stats_in=None,
                stats_out=None):
    HID = 4096
    wfc_h = nc.dram_tensor("L0_ffn_w_fc", [HID, D], F32, kind="ExternalInput").ap()
    bfc_h = nc.dram_tensor("L0_ffn_b_fc", [HID], F32, kind="ExternalInput").ap()
    wpr_h = nc.dram_tensor("L0_ffn_w_proj", [D, HID], F32, kind="ExternalInput").ap()
    bpr_h = nc.dram_tensor("L0_ffn_b_proj", [D], F32, kind="ExternalInput").ap()

    A = lambda n, s, d: P.salloc(n, s, d)
    q = QS(P, "p2q")
    wslab = [A(f"p2_wslab{j}", [128, D], F32) for j in range(2)]
    wq_bf = A("p2_wqbf", [128, D], BF16)
    wfcT = A("p2_wfcT", [128, 8 * HID], BF16)
    wprT = A("p2_wprT", [128, 32 * D], BF16)
    bfc_bf = A("p2_bfc", [1, HID], BF16)
    bpr_bf = A("p2_bpr", [1, D], BF16)
    xin = [A(f"p2_xin{j}", [128, D], F32) for j in range(2)]
    xout = [A(f"p2_xout{j}", [128, D], F32) for j in range(2)]
    xn = q.fm8
    xq_bf = A("p2_xqbf", [128, D], BF16)
    xqT = A("p2_xqT", [128, D], BF16)
    hg = A("p2_hg", [128, D], F32)
    hg2 = hg
    hq_bf = A("p2_hqbf", [128, D], BF16)
    hq_bf2 = A("p2_hqbf2", [128, D], BF16)
    hqT = [A("p2_hqT0", [128, 4 * D], BF16)]

    PS = lambda n, s, d: P.palloc(n, s, d)
    b = [PS(f"p2_b{i}", [128, 512], F32) for i in range(6)]
    pT = PS("p2_pT", [128, 1024], BF16)
    pT2 = PS("p2_pT2", [128, 1024], BF16)
    q2 = Ctx()
    q2.y = xout[0]
    q2.g = xout[1]
    q2.i32 = q.i32
    q2.fm8 = q.i32.bitcast(F32)
    q2.am = hg[:, 0:32]
    q2.inv = hg[:, 32:64]
    q2.sc = hg[:, 64:96]
    qs2 = [q, q2]
    wqs = [wq_bf, xq_bf]
    pTs = [pT, pT2]

    eps = cx.cols[:, 0:1]

    P.dma("gpsimd", bfc_bf, bfc_h.rearrange("(o n) -> o n", o=1), stream="weights")
    P.dma("gpsimd", bpr_bf, bpr_h.rearrange("(o n) -> o n", o=1), stream="weights")

    for s in range(32):
        par = s % 2
        sl = wslab[par]
        P.dma("gpsimd", sl[:, 0:D], wfc_h.rearrange("(s p) n -> s p n", p=128)[s],
              stream=f"w{par}")
        quant_chain(P, cx, qs2[par], sl[:, 0:D], wqs[par][:, 0:D])
        for k in range(8):
            P.transpose(pTs[par][:, k * 128:(k + 1) * 128],
                        wqs[par][:, k * 128:(k + 1) * 128], cx.ident)
        dst = wfcT.rearrange("p (k n) -> p k n", k=8)[:, :, 128 * s:128 * s + 128]
        P.v_copy(dst, pTs[par].rearrange("p (k n) -> p k n", k=8))
    for s in range(8):
        for c4 in range(4):
            par = (s * 4 + c4) % 2
            sl = wslab[par]
            P.dma("gpsimd", sl,
                  wpr_h.rearrange("(s p) (c n) -> s p c n", p=128, n=D)[s, :, c4],
                  stream=f"w{par}")
            quant_chain(P, cx, qs2[par], sl, wqs[par][:, 0:D])
            for k in range(8):
                P.transpose(pTs[par][:, k * 128:(k + 1) * 128],
                            wqs[par][:, k * 128:(k + 1) * 128], cx.ident)
            dst = wprT.rearrange("p (k n) -> p k n", k=32)[:, c4 * 8:(c4 + 1) * 8,
                                                          128 * s:128 * s + 128]
            P.v_copy(dst, pTs[par].rearrange("p (k n) -> p k n", k=8))

    rstd_all, negmr_all = ln_prepass(P, nc, cx, x_hbm, n_tiles, "p2pre",
                                     bufs=(xin[0], xin[1], q.g),
                                     stats_in=stats_in)
    x_t = x_hbm.rearrange("(n p) d -> n p d", p=128)
    o_t = out_hbm.rearrange("(n p) d -> n p d", p=128)

    for i in range(n_tiles + 1):
        if i < n_tiles:
            j2 = i % 2
            xi = xin[j2]
            P.dma("sync", xi, x_t[i], stream=f"xin{j2}")
            rstd, negmr = rstd_all[:, i:i + 1], negmr_all[:, i:i + 1]
            P.s_act(xn, xi, AF.Identity, bias=negmr, scale=rstd)
            quant_chain(P, cx, q, xn, xq_bf)
            for k in range(8):
                P.transpose(pT[:, k * 128:(k + 1) * 128],
                            xq_bf[:, k * 128:(k + 1) * 128], cx.ident)
            P.s_copy(xqT, pT)
            hT = hqT[0]
            hgs = [hg, hg2]
            hqs = [hq_bf, hq_bf2]
            for j in range(4):
                pj = (b[0], b[1])
                for half in range(2):
                    col0 = j * D + half * 512
                    pairs = [(xqT[:, k * 128:(k + 1) * 128],
                              wfcT[:, k * HID + col0: k * HID + col0 + 512])
                             for k in range(8)]
                    pairs.append((cx.ones_bf, bfc_bf[0:1, col0:col0 + 512]))
                    P.mm_acc(pj[half], pairs)
                    P.s_act(hgs[j % 2][:, half * 512:(half + 1) * 512], pj[half],
                            AF.Gelu)
                quant_chain(P, cx, q, hgs[j % 2], hqs[j % 2])
                if j >= 1:
                    for k in range(8):
                        P.transpose(pT[:, k * 128:(k + 1) * 128],
                                    hqs[(j - 1) % 2][:, k * 128:(k + 1) * 128],
                                    cx.ident)
                    P.s_copy(hT[:, (j - 1) * D:j * D], pT)
            for k in range(8):
                P.transpose(pT[:, k * 128:(k + 1) * 128],
                            hqs[1][:, k * 128:(k + 1) * 128], cx.ident)
            P.s_copy(hT[:, 3 * D:4 * D], pT)
            yb = (b[2], b[3]) if j2 == 0 else (b[4], b[5])
            for half in range(2):
                pairs = [(hT[:, kk * 128:(kk + 1) * 128],
                          wprT[:, kk * D + half * 512: kk * D + half * 512 + 512])
                         for kk in range(32)]
                pairs.append((cx.ones_bf, bpr_bf[0:1, half * 512:(half + 1) * 512]))
                P.mm_acc(yb[half], pairs)
        if i > 0:
            jp = (i - 1) % 2
            ybp = (b[2], b[3]) if jp == 0 else (b[4], b[5])
            xo = xout[jp]
            for half in range(2):
                sl = slice(half * 512, (half + 1) * 512)
                P.v_tt(xo[:, sl], ybp[half], xin[jp][:, sl], ALU.add)
            P.dma("sync", o_t[i - 1], xo, stream=f"xo{jp}")
            if stats_out is not None:
                emit_out_stats(P, stats_out, xo, xin[jp], i - 1)



def build_pass1(P, nc, cx, x_hbm, out_hbm, n_tiles, stats_out=None):
    H, DK, DV = 8, 128, 128
    win = {}
    for nm in ("Wq", "Wk", "Wv", "Wg", "Wo"):
        h = nc.dram_tensor(f"L0_attn_{nm}", [D, D], F32, kind="ExternalInput").ap()
        win[nm] = load_weight_bf16(P, nc, f"p1_{nm}", h, D, D)

    A = lambda n, s, d: P.salloc(n, s, d)
    xin = [A(f"p1_xin{j}", [128, D], F32) for j in range(2)]
    xout = [A(f"p1_xout{j}", [128, D], F32) for j in range(2)]
    stat = A("p1_stat", [128, 8], F32)
    scratch = A("p1_scratch", [128, D], F32)
    xn_bf = A("p1_xnbf", [128, D], BF16)
    ecn_bf = A("p1_ecnbf", [128, D], BF16)
    xnT = A("p1_xnT", [128, D], BF16)
    qsc = A("p1_qsc", [128, D], F32)
    ksc = A("p1_ksc", [128, D], F32)
    eCn = A("p1_eCn", [128, D], F32)
    eCp = A("p1_eCp", [128, D], F32)
    sp = A("p1_sp", [128, D], F32)
    invq = A("p1_invq", [128, 8], F32)
    invk = A("p1_invk", [128, 8], F32)
    qt_bf = A("p1_qtbf", [128, D], BF16)
    kt_bf = A("p1_ktbf", [128, D], BF16)
    qT = A("p1_qT", [128, D], BF16)
    kT = A("p1_kT", [128, D], BF16)
    k2 = A("p1_k2", [64, 2 * D], BF16)
    v2 = A("p1_v2", [64, 2 * D], BF16)
    am_all = A("p1_amall", [64, 8 * 64], BF16)
    pbs1 = A("p1_pbs", [128, 1024], F32)
    o_sb = A("p1_o", [128, D], BF16)
    oT = [A("p1_oT0", [128, D], BF16), A("p1_oT1", [128, D], BF16)]
    S_f = A("p1_Sf", [128, H * DV], F32)
    S_b = A("p1_Sb", [128, H * DV], BF16)
    Bcol = A("p1_Bcol", [128, 16], F32)

    PS = lambda n, s, d: P.palloc(n, s, d)
    b = [PS(f"p1_b{i}", [128, 512], F32) for i in range(7)]
    pT = PS("p1_pT", [128, 1024], BF16)

    eps = cx.cols[:, 0:1]
    eps2 = cx.cols[:, 1:2]
    one = cx.cols[:, 2:3]

    P.v_memset(S_f, 0.0)
    P.v_memset(S_b, 0.0)

    x_t = x_hbm.rearrange("(n p) d -> n p d", p=128)
    o_t = out_hbm.rearrange("(n p) d -> n p d", p=128)

    for i in range(n_tiles + 1):
      if i < n_tiles:
        xi = xin[i % 2]
        P.dma("sync", xi, x_t[i], stream=f"xin{i % 2}")
        rstd, negmr = ln_stats(P, xi, scratch, stat, eps)
        P.s_act(xn_bf, xi, AF.Identity, bias=negmr, scale=rstd)
        transpose_1024(P, cx, xn_bf, pT, xnT)
        proj_1024(P, (b[0], b[1]), xnT, win["Wq"])
        proj_1024(P, (b[2], b[3]), xnT, win["Wk"])
        for half in range(2):
            P.s_act(scratch[:, half * 512:(half + 1) * 512], b[half], AF.Square)
        P.v_red(invq.rearrange("p (b o) -> p b o", o=1),
                scratch.rearrange("p (b k) -> p b k", k=128), ALU.add)
        for half in range(2):
            P.s_act(scratch[:, half * 512:(half + 1) * 512], b[2 + half], AF.Square)
        P.v_red(invk.rearrange("p (b o) -> p b o", o=1),
                scratch.rearrange("p (b k) -> p b k", k=128), ALU.add)
        for t in (invq, invk):
            P.v_ts(t, t, eps2, ALU.max)
            P.s_act(t, t, AF.Ln)
            P.s_act(t, t, AF.Exp, scale=-0.5)
        proj_1024(P, (b[4], b[5]), xnT, win["Wg"])
        for half in range(2):
            P.s_act(sp[:, half * 512:(half + 1) * 512], b[4 + half], AF.Exp, scale=-1.0)
        P.s_act(sp, sp, AF.Ln, bias=one)
        for half in range(2):
            sl = slice(half * 512, (half + 1) * 512)
            P.mm(b[6], cx.cum64, sp[:, sl])
            P.s_act(eCn[:, sl], b[6], AF.Exp, scale=-1.0)
            P.s_act(eCp[:, sl], b[6], AF.Exp)
        for h in range(H):
            blk = slice(h * 128, (h + 1) * 128)
            P.v_ts(qsc[:, blk], eCn[:, blk], invq[:, h:h + 1], ALU.mult)
            P.v_ts(ksc[:, blk], eCp[:, blk], invk[:, h:h + 1], ALU.mult)
        for half in range(2):
            sl = slice(half * 512, (half + 1) * 512)
            P.v_tt(qt_bf[:, sl], b[half], qsc[:, sl], ALU.mult)
            P.v_tt(kt_bf[:, sl], b[2 + half], ksc[:, sl], ALU.mult)
        proj_1024(P, (b[0], b[1]), xnT, win["Wv"])
        for c in range(2):
            for half in range(2):
                P.s_copy(v2[:, c * D + half * 512: c * D + (half + 1) * 512],
                         b[half][c * 64:(c + 1) * 64, :])
        for c in range(2):
            P.v_copy(k2[:, c * D:(c + 1) * D], kt_bf[c * 64:(c + 1) * 64, :])
        transpose_1024(P, cx, qt_bf, pT, qT)
        transpose_1024(P, cx, kt_bf, pT, kT)
        P.v_copy(ecn_bf, eCn)
        for c in range(2):
            for h in range(H):
                P.mm(b[6][:, c * 8 + h: c * 8 + h + 1],
                     ecn_bf[:, h * 128:(h + 1) * 128], cx.onehot[:, c:c + 1])
        P.v_copy(Bcol, b[6][:, 0:16])
        for c in range(2):
            for h in range(H):
                tcols = slice(h * 128 + c * 64, h * 128 + c * 64 + 64)
                pa = b[4 + h % 2][0:64, (h // 2) * 64:(h // 2) * 64 + 64]
                P.mm(pa, kT[:, tcols], qT[:, tcols])
                P.v_tt(am_all[:, h * 64:(h + 1) * 64], pa, cx.mask_incl64, ALU.mult)
            for h in range(H):
                tcols = slice(h * 128 + c * 64, h * 128 + c * 64 + 64)
                hblk = slice(h * DV, (h + 1) * DV)
                po = b[2 + h // 4][0:64, (h % 4) * 128:(h % 4) * 128 + 128]
                P.mm_acc(po, [(am_all[:, h * 64:(h + 1) * 64],
                               v2[:, c * D + h * 128: c * D + (h + 1) * 128]),
                              (qT[:, tcols], S_b[:, hblk])])
            for h in range(H):
                hblk = slice(h * DV, (h + 1) * DV)
                po = b[2 + h // 4][0:64, (h % 4) * 128:(h % 4) * 128 + 128]
                P.s_copy(o_sb[c * 64:(c + 1) * 64, hblk], po)
            for h in range(H):
                cblk = slice(c * D + h * DV, c * D + (h + 1) * DV)
                pp = b[4 + h // 4][:, (h % 4) * 128:(h % 4) * 128 + 128]
                P.mm(pp, k2[:, cblk], v2[:, cblk])
            bc3 = Bcol[:, c * 8:(c + 1) * 8].rearrange("p (b o) -> p b o", o=1)\
                .broadcast_to([128, 8, 128])
            S3 = S_f.rearrange("p (b k) -> p b k", k=128)
            P.v_tt(S3, S3, bc3, ALU.mult)
            for half in range(2):
                P.v_tt(pbs1.rearrange("p (b k) -> p b k", k=128)[:, half * 4:(half + 1) * 4, :],
                       b[4 + half].rearrange("p (b k) -> p b k", k=128),
                       bc3[:, half * 4:(half + 1) * 4, :], ALU.mult)
            P.v_tt(S_f, S_f, pbs1, ALU.add)
            P.s_copy(S_b, S_f)
        transpose_1024(P, cx, o_sb, pT, oT[i % 2])
      if i > 0:
        jp = (i - 1) % 2
        xo = xout[jp]
        proj_1024(P, (b[0], b[1]), oT[jp], win["Wo"])
        for half in range(2):
            sl = slice(half * 512, (half + 1) * 512)
            P.v_tt(xo[:, sl], b[half], xin[jp][:, sl], ALU.add)
        P.dma("sync", o_t[i - 1], xo, stream=f"xo{jp}")
        if stats_out is not None:
            emit_out_stats(P, stats_out, xo, xin[jp], i - 1)





def build_pass3(P, nc, cx, x_hbm, out_hbm, n_tiles, stats_in=None,
                stats_out=None):
    H, N = 16, 64
    TM, TD = 32, 64
    A = lambda n, s, d: P.salloc(n, s, d)
    dram = lambda n, s: nc.dram_tensor(n, s, F32, kind="ExternalInput").ap()

    win = {}
    for nm in ("Wr", "Wk", "Wv", "Wg", "Wo"):
        win[nm] = load_weight_bf16(P, nc, f"p3_{nm}", dram(f"L1_attn_{nm}", [D, D]), D, D)
    tmw1 = load_weight_bf16(P, nc, "p3_tmw1", dram("L1_attn_tm_w1", [D, 5 * TM]), D, 5 * TM)
    tdw1 = load_weight_bf16(P, nc, "p3_tdw1", dram("L1_attn_td_w1", [D, TD]), D, TD)
    tmw2 = A("p3_tmw2", [TM, 5 * D], BF16)
    P.dma("gpsimd", tmw2.rearrange("d (f c) -> d f c", f=5),
          dram("L1_attn_tm_w2", [5, TM, D]).rearrange("f d c -> d f c"), stream="w0")
    tdw2 = A("p3_tdw2", [TD, D], BF16)
    P.dma("gpsimd", tdw2, dram("L1_attn_td_w2", [TD, D]), stream="w1")

    def rep_row(name, hname, dt):
        t = A(name, [128, D], dt)
        h = dram(hname, [D])
        P.dma("gpsimd" if dt == BF16 else "sync",
              t, h.rearrange("(o n) -> o n", o=1).broadcast_to([128, D]),
              stream="w0" if dt == BF16 else "consts")
        return t

    def plain_row(name, hname):
        t = A(name, [1, D], BF16)
        P.dma("gpsimd", t, dram(hname, [D]).rearrange("(o n) -> o n", o=1), stream="w0")
        return t

    maax = rep_row("p3_maax", "L1_attn_time_maa_x", BF16)
    maar = {}
    for nm in ("w", "k", "v", "r", "g"):
        maar[nm] = plain_row(f"p3_maa{nm}", f"L1_attn_time_maa_{nm}")
    g0 = rep_row("p3_g0", "L1_attn_ln0_g", F32)
    b0r = rep_row("p3_b0r", "L1_attn_ln0_b", F32)
    g1 = rep_row("p3_g1", "L1_attn_ln1_g", BF16)
    b1r = rep_row("p3_b1r", "L1_attn_ln1_b", BF16)
    lnxg = rep_row("p3_lnxg", "L1_attn_lnx_g", BF16)
    lnxb = rep_row("p3_lnxb", "L1_attn_lnx_b", BF16)
    u_rep = A("p3_urep", [128, D], BF16)
    P.dma("gpsimd", u_rep,
          dram("L1_attn_time_faaaa", [H, N]).rearrange("h n -> (h n)")
          .rearrange("(o n) -> o n", o=1).broadcast_to([128, D]), stream="w0")
    tdec = A("p3_tdec", [1, D], BF16)
    P.dma("gpsimd", tdec, dram("L1_attn_time_decay", [D]).rearrange("(o n) -> o n", o=1),
          stream="w1")

    xin = A("p3_xin", [128, D], F32)
    xout = [A(f"p3_xout{j}", [128, D], F32) for j in range(2)]
    xL = [A(f"p3_xL{j}", [128, D], F32) for j in range(2)]
    stat = A("p3_stat", [128, 12], F32)
    scratch = A("p3_scratch", [128, D], F32)
    ruk = scratch
    xa_bf = A("p3_xabf", [128, D], BF16)
    ecn_bf = xa_bf
    carry = A("p3_carry", [1, D], BF16)
    dxa = A("p3_dxa", [128, D], BF16)
    og_bf = A("p3_og", [128, D], BF16)
    shf = og_bf
    xxx = og_bf
    xxxT = A("p3_xxxT", [128, D], BF16)
    xcT = xxxT
    lora_bf = A("p3_lora", [128, 5 * TM], BF16)
    loraT = A("p3_loraT", [TM, 5 * 128], BF16)
    mix_all = A("p3_mixall", [128, 5 * D], BF16)
    kt_bf = mix_all[:, 0 * D:1 * D]
    k_bf = mix_all[:, 1 * D:2 * D]
    v_bf = mix_all[:, 2 * D:3 * D]
    r_bf = mix_all[:, 3 * D:4 * D]
    g_bf = mix_all[:, 4 * D:5 * D]
    t64 = A("p3_t64", [128, TD], BF16)
    t64T = A("p3_t64T", [TD, 128], BF16)
    o_sb = A("p3_o", [128, D], F32)
    ew = o_sb
    edec = A("p3_edec", [128, D], BF16)
    rt_bf = A("p3_rtbf", [128, D], BF16)
    rT = A("p3_rT", [N, H * 128], BF16)
    kT = A("p3_kT", [N, H * 128], BF16)
    Dcol = A("p3_Dcol", [128, H], F32)
    am_all = A("p3_amall", [128, 8 * 128], BF16)
    pbs = A("p3_pbs", [N, 8 * N], F32)
    pb3 = pbs.rearrange("p (b k) -> p b k", k=N)
    ogT = [A(f"p3_ogT{j}", [128, D], BF16) for j in range(2)]
    S_f = A("p3_Sf", [N, H * N], F32)
    S_b = A("p3_Sb", [N, H * N], BF16)
    Bcol = A("p3_Bcol", [N, H], F32)
    gnm = A("p3_gnm", [128, 16], F32)
    gnr = A("p3_gnr", [128, 16], F32)

    PS = lambda n, s, d: P.palloc(n, s, d)
    b = [PS(f"p3_ps{i}", [128, 512], F32) for i in range(6)]
    pTs = [PS("p3_pTa", [128, 1024], BF16), PS("p3_pTb", [128, 1024], BF16)]
    tctr = [0]

    def next_pT():
        tctr[0] += 1
        return pTs[tctr[0] % 2]

    eps = cx.cols[:, 0:1]
    P.v_memset(S_f, 0.0)
    P.v_memset(S_b, 0.0)
    P.v_memset(carry, 0.0)

    x_t = x_hbm.rearrange("(n p) d -> n p d", p=128)
    o_t = out_hbm.rearrange("(n p) d -> n p d", p=128)
    f1_all, f2_all = ln_prepass_double(P, nc, cx, x_hbm, n_tiles, "p3pre",
                                       bufs=(xin, xin, scratch),
                                       stats_in=stats_in)

    for it in range(n_tiles + 2):
        if it >= 2:
            ic = it - 2
            jc = ic % 2
            xo = xout[jc]
            proj_1024(P, (b[0], b[1]), ogT[jc], win["Wo"])
            for half in range(2):
                sl = slice(half * 512, (half + 1) * 512)
                P.v_tt(xo[:, sl], b[half], xL[jc][:, sl], ALU.add)
            P.dma("sync", o_t[ic], xo, stream=f"xo{jc}")
            if stats_out is not None:
                emit_out_stats(P, stats_out, xo, xL[jc], ic)
        if 1 <= it <= n_tiles:
            ib = it - 1
            jb = ib % 2
            for h in range(H):
                P.mm(b[4][0:N, h:h + 1], ecn_bf[:, h * N:(h + 1) * N],
                     cx.onehot[:, 1:2])
            P.v_copy(Bcol, b[4][0:N, 0:H])
            for grp in range(2):
                h0 = grp * 8
                for hh in range(8):
                    h = h0 + hh
                    pa = b[4 + hh % 2]
                    P.mm(pa[:, 0:128], kT[:, h * 128:(h + 1) * 128],
                         rT[:, h * 128:(h + 1) * 128])
                    P.v_tt(am_all[:, hh * 128:(hh + 1) * 128], pa[:, 0:128],
                           cx.mask_strict128, ALU.mult)
                for hh in range(8):
                    h = h0 + hh
                    hn = slice(h * N, (h + 1) * N)
                    P.mm_acc(b[4][:, hh * N:(hh + 1) * N],
                             [(am_all[:, hh * 128:(hh + 1) * 128], v_bf[:, hn]),
                              (rT[:, h * 128:(h + 1) * 128], S_b[:, hn])])
                for hh in range(8):
                    h = h0 + hh
                    hn = slice(h * N, (h + 1) * N)
                    P.v_stt(o_sb[:, hn], v_bf[:, hn], Dcol[:, h:h + 1],
                            b[4][:, hh * N:(hh + 1) * N], ALU.mult, ALU.add)
                for hh in range(8):
                    h = h0 + hh
                    hn = slice(h * N, (h + 1) * N)
                    P.mm(b[5][0:N, hh * N:(hh + 1) * N], kt_bf[:, hn], v_bf[:, hn])
                gsl = slice(h0 * N, (h0 + 8) * N)
                bc3 = Bcol[:, h0:h0 + 8].rearrange("p (b o) -> p b o", o=1)\
                    .broadcast_to([N, 8, N])
                S3 = S_f[:, gsl].rearrange("p (b k) -> p b k", k=N)
                P.v_tt(S3, S3, bc3, ALU.mult)
                P.v_tt(pb3, b[5][0:N, 0:8 * N].rearrange("p (b k) -> p b k", k=N),
                       bc3, ALU.mult)
                P.v_tt(S_f[:, gsl], S_f[:, gsl], pbs, ALU.add)
                P.s_copy(S_b[:, gsl], S_f[:, gsl])
            P.s_act(scratch, o_sb, AF.Square)
            P.v_red(gnm.rearrange("p (b o) -> p b o", o=1),
                    o_sb.rearrange("p (b k) -> p b k", k=64), ALU.add)
            P.v_red(gnr.rearrange("p (b o) -> p b o", o=1),
                    scratch.rearrange("p (b k) -> p b k", k=64), ALU.add)
            P.v_ts(gnm, gnm, 1.0 / 64, ALU.mult)
            P.v_ts(gnr, gnr, 1.0 / 64, ALU.mult)
            P.v_tt(scratch[:, 0:16], gnm, gnm, ALU.mult)
            P.v_tt(gnr, gnr, scratch[:, 0:16], ALU.subtract)
            P.v_ts(gnr, gnr, eps, ALU.add)
            gnrec = scratch[:, 48:64]
            rsqrt_col(P, cx, gnrec, gnr, scratch[:, 16:32], scratch[:, 32:48])
            gm_bc = gnm.rearrange("p (b o) -> p b o", o=1).broadcast_to([128, 16, 64])
            gr_bc = gnrec.rearrange("p (b o) -> p b o", o=1).broadcast_to([128, 16, 64])
            o3 = o_sb.rearrange("p (b k) -> p b k", k=64)
            P.v_tt(o3, o3, gm_bc, ALU.subtract)
            P.v_tt(o3, o3, gr_bc, ALU.mult)
            P.v_tt(o_sb, o_sb, lnxg, ALU.mult)
            P.v_tt(o_sb, o_sb, lnxb, ALU.add)
            P.v_tt(og_bf, o_sb, g_bf, ALU.mult)
            transpose_1024(P, cx, og_bf, next_pT(), ogT[jb])
        if it < n_tiles:
            i = it
            ja = i % 2
            P.dma("sync", xin, x_t[i], stream="xin0")
            P.s_act(scratch, xin, AF.Identity, bias=f2_all[:, i:i + 1],
                    scale=f1_all[:, i:i + 1])
            P.v_tt(xL[ja], scratch, g0, ALU.mult)
            P.v_tt(xL[ja], xL[ja], b0r, ALU.add)
            rstd, negmr = ln_stats_nolut(P, cx, xL[ja], scratch, stat, eps)
            P.s_act(scratch, xL[ja], AF.Identity, bias=negmr, scale=rstd)
            P.v_tt(dxa, scratch, g1, ALU.mult)
            P.v_tt(xa_bf, dxa, b1r, ALU.add)
            P.dma("gpsimd", shf[1:128, :], xa_bf[0:127, :], stream="shift")
            P.dma("gpsimd", shf[0:1, :], carry, stream="shift")
            P.dma("gpsimd", carry, xa_bf[127:128, :], stream="shift")
            P.v_tt(dxa, shf, xa_bf, ALU.subtract)
            P.v_tt(xxx, dxa, maax, ALU.mult)
            P.v_tt(xxx, xxx, xa_bf, ALU.add)
            transpose_1024(P, cx, xxx, next_pT(), xxxT)
            pairs = [(xxxT[:, k * 128:(k + 1) * 128],
                      tmw1[:, k * 5 * TM:(k + 1) * 5 * TM]) for k in range(8)]
            P.mm_acc(b[2][:, 0:5 * TM], pairs)
            P.s_act(lora_bf, b[2][:, 0:5 * TM], AF.Tanh)
            pL = next_pT()
            for c in range(5):
                P.transpose(pL[0:TM, c * 128:(c + 1) * 128],
                            lora_bf[:, c * TM:(c + 1) * TM], cx.ident)
            P.v_copy(loraT, pL[0:TM, 0:5 * 128])
            proj_of = {"r": ("Wr", r_bf, "copy"), "k": ("Wk", k_bf, "copy"),
                       "v": ("Wv", v_bf, "copy"), "g": ("Wg", g_bf, "silu")}
            variants = ("w", "k", "v", "r", "g")
            for ci, cn in enumerate(variants):
                for half in range(2):
                    pr = [(loraT[:, ci * 128:(ci + 1) * 128],
                           tmw2[:, ci * D + half * 512: ci * D + half * 512 + 512]),
                          (cx.ones_bf, maar[cn][0:1, half * 512:(half + 1) * 512])]
                    P.mm_acc(b[ci % 2][:, 0:512] if half == 0 else b[2 + ci % 2][:, 0:512], pr)
                    dst = mix_all[:, ci * D + half * 512: ci * D + (half + 1) * 512]
                    P.s_copy(dst, b[ci % 2] if half == 0 else b[2 + ci % 2])
            m3 = mix_all.rearrange("p (c d) -> p c d", d=D)
            dxa3 = dxa.rearrange("p (o d) -> p o d", o=1).broadcast_to([128, 5, D])
            xa3 = xa_bf.rearrange("p (o d) -> p o d", o=1).broadcast_to([128, 5, D])
            P.v_tt(m3, m3, dxa3, ALU.mult)
            P.v_tt(m3, m3, xa3, ALU.add)
            for ci, cn in enumerate(variants):
                xc_bf = mix_all[:, ci * D:(ci + 1) * D]
                transpose_1024(P, cx, xc_bf, next_pT(), xcT)
                if cn == "w":
                    pr = [(xcT[:, k * 128:(k + 1) * 128],
                           tdw1[:, k * TD:(k + 1) * TD]) for k in range(8)]
                    P.mm_acc(b[2][:, 0:TD], pr)
                    P.s_act(t64, b[2][:, 0:TD], AF.Tanh)
                    pW = next_pT()
                    P.transpose(pW[0:TD, 0:128], t64, cx.ident)
                    P.v_copy(t64T, pW[0:TD, 0:128])
                    for half in range(2):
                        pr = [(t64T[:, 0:128],
                               tdw2[:, half * 512:(half + 1) * 512]),
                              (cx.ones_bf, tdec[0:1, half * 512:(half + 1) * 512])]
                        P.mm_acc(b[half], pr)
                    for half in range(2):
                        P.s_act(ew[:, half * 512:(half + 1) * 512], b[half], AF.Exp)
                else:
                    wname, dst, mode = proj_of[cn]
                    proj_1024(P, (b[2], b[3]), xcT, win[wname])
                    for half in range(2):
                        sl2 = slice(half * 512, (half + 1) * 512)
                        if mode == "silu":
                            P.s_act(dst[:, sl2], b[2 + half], AF.Silu)
                        else:
                            P.s_copy(dst[:, sl2], b[2 + half])
            P.v_tt(ruk[:, 0:D], r_bf, k_bf, ALU.mult)
            P.v_tt(ruk[:, 0:D], ruk[:, 0:D], u_rep, ALU.mult)
            P.v_red(Dcol.rearrange("p (b o) -> p b o", o=1),
                    ruk.rearrange("p (b k) -> p b k", k=64), ALU.add)
            for half in range(2):
                P.mm(b[2 + half], cx.mask_strict128, ew[:, half * 512:(half + 1) * 512])
                P.s_act(edec[:, half * 512:(half + 1) * 512], b[2 + half], AF.Exp,
                        scale=-1.0)
            P.v_tt(rt_bf, r_bf, edec, ALU.mult)
            for half in range(2):
                P.mm(b[2 + half], cx.cum128, ew[:, half * 512:(half + 1) * 512])
                P.s_act(edec[:, half * 512:(half + 1) * 512], b[2 + half], AF.Exp)
            P.v_tt(kt_bf, k_bf, edec, ALU.mult)
            for half in range(2):
                P.s_act(ecn_bf[:, half * 512:(half + 1) * 512], b[2 + half], AF.Exp,
                        scale=-1.0)
            for grp in range(2):
                pR = next_pT()
                for hh in range(8):
                    h = grp * 8 + hh
                    P.transpose(pR[0:N, hh * 128:(hh + 1) * 128],
                                rt_bf[:, h * N:(h + 1) * N], cx.ident)
                P.s_copy(rT[:, grp * 1024:(grp + 1) * 1024], pR[0:N, 0:1024])
                pK = next_pT()
                for hh in range(8):
                    h = grp * 8 + hh
                    P.transpose(pK[0:N, hh * 128:(hh + 1) * 128],
                                kt_bf[:, h * N:(h + 1) * N], cx.ident)
                P.s_copy(kT[:, grp * 1024:(grp + 1) * 1024], pK[0:N, 0:1024])



def build_pass4(P, nc, cx, xmid_hbm, x3_hbm, out_hbm, n_tiles, stats_in=None,
                stats_out=None):
    FF = 3584
    A = lambda n, s, d: P.salloc(n, s, d)
    dram = lambda n, s: nc.dram_tensor(n, s, F32, kind="ExternalInput").ap()
    wk = load_weight_bf16(P, nc, "p4_wk", dram("L1_attn_Wk_ffn", [D, FF]), D, FF)
    wv = load_weight_bf16(P, nc, "p4_wv", dram("L1_attn_Wv_ffn", [FF, D]), FF, D)
    wr = load_weight_bf16(P, nc, "p4_wr", dram("L1_attn_Wr_ffn", [D, D]), D, D)
    g2 = A("p4_g2", [128, D], F32)
    P.dma("sync", g2, dram("L1_attn_ln2_g", [D]).rearrange("(o n) -> o n", o=1)
          .broadcast_to([128, D]), stream="consts")
    b2 = A("p4_b2", [128, D], F32)
    P.dma("sync", b2, dram("L1_attn_ln2_b", [D]).rearrange("(o n) -> o n", o=1)
          .broadcast_to([128, D]), stream="consts")
    maak = A("p4_maak", [128, D], BF16)
    P.dma("gpsimd", maak, dram("L1_attn_ffn_maa_k", [D]).rearrange("(o n) -> o n", o=1)
          .broadcast_to([128, D]), stream="w0")
    maar = A("p4_maar", [128, D], BF16)
    P.dma("gpsimd", maar, dram("L1_attn_ffn_maa_r", [D]).rearrange("(o n) -> o n", o=1)
          .broadcast_to([128, D]), stream="w0")

    xin = [A(f"p4_xin{j}", [128, D], F32) for j in range(2)]
    x3in = [A(f"p4_x3in{j}", [128, D], F32) for j in range(2)]
    xout = [A(f"p4_xout{j}", [128, D], F32) for j in range(2)]
    scratch = A("p4_scratch", [128, D], F32)
    relu_s = A("p4_relus", [128, D], BF16)
    xf_bf = A("p4_xfbf", [128, D], BF16)
    shf = A("p4_shf", [128, D], BF16)
    carry = A("p4_carry", [1, D], BF16)
    dxf = A("p4_dxf", [128, D], BF16)
    xv_bf = shf
    xvTk = [A(f"p4_xvTk{j}", [128, D], BF16) for j in range(2)]
    xvTr = [A(f"p4_xvTr{j}", [128, D], BF16) for j in range(2)]
    kk_bf = A("p4_kk", [128, D], BF16)
    kkT = A("p4_kkT", [128, FF], BF16)
    sig = None
    sigs = [A("p4_sig0", [128, D], F32), A("p4_sig1", [128, D], F32)]

    PS = lambda n, s, d: P.palloc(n, s, d)
    b = [PS(f"p4_ps{i}", [128, 512], F32) for i in range(6)]
    pT = PS("p4_pT", [128, 1024], BF16)
    pT2 = PS("p4_pT2", [128, 1024], BF16)

    eps = cx.cols[:, 0:1]
    P.v_memset(carry, 0.0)

    xm_t = xmid_hbm.rearrange("(n p) d -> n p d", p=128)
    x3_t = x3_hbm.rearrange("(n p) d -> n p d", p=128)
    o_t = out_hbm.rearrange("(n p) d -> n p d", p=128)

    rstd_all, negmr_all = ln_prepass(P, nc, cx, xmid_hbm, n_tiles, "p4pre",
                                     bufs=(xin[0], xin[1], scratch),
                                     stats_in=stats_in)
    for it in range(n_tiles + 2):
      if it >= 2:
        jp = (it - 2) % 2
        ybp = (b[0], b[1]) if jp == 0 else (b[2], b[3])
        xo = xout[jp]
        for half in range(2):
            sl = slice(half * 512, (half + 1) * 512)
            P.v_tt(xo[:, sl], ybp[half], sigs[jp][:, sl], ALU.mult)
            P.v_tt(xo[:, sl], xo[:, sl], xin[jp][:, sl], ALU.add)
            P.v_tt(xo[:, sl], xo[:, sl], x3in[jp][:, sl], ALU.add)
        P.dma("sync", o_t[it - 2], xo, stream=f"xo{jp}")
        if stats_out is not None:
            emit_out_stats(P, stats_out, xo, x3in[jp], it - 2)
      if it < n_tiles:
        j2 = it % 2
        xi = xin[j2]
        x3 = x3in[j2]
        P.dma("sync", xi, xm_t[it], stream=f"xin{j2}")
        P.dma("sync", x3, x3_t[it], stream=f"x3in{j2}")
        rstd, negmr = rstd_all[:, it:it + 1], negmr_all[:, it:it + 1]
        P.s_act(scratch, xi, AF.Identity, bias=negmr, scale=rstd)
        P.v_tt(dxf, scratch, g2, ALU.mult)
        P.v_tt(xf_bf, dxf, b2, ALU.add)
        P.dma("gpsimd", shf[1:128, :], xf_bf[0:127, :], stream="shift")
        P.dma("gpsimd", shf[0:1, :], carry, stream="shift")
        P.dma("gpsimd", carry, xf_bf[127:128, :], stream="shift")
        P.v_tt(dxf, shf, xf_bf, ALU.subtract)
        P.v_tt(xv_bf, dxf, maak, ALU.mult)
        P.v_tt(xv_bf, xv_bf, xf_bf, ALU.add)
        transpose_1024(P, cx, xv_bf, pT2, xvTk[j2])
        P.v_tt(xv_bf, dxf, maar, ALU.mult)
        P.v_tt(xv_bf, xv_bf, xf_bf, ALU.add)
        transpose_1024(P, cx, xv_bf, pT2, xvTr[j2])
      if 1 <= it <= n_tiles:
        im = it - 1
        jm = im % 2
        xvT = xvTk[jm]
        for jc in range(4):
            w0 = jc * 1024
            wid = min(FF - w0, 1024)
            nh = (wid + 511) // 512
            pj = (b[0], b[1]) if jm == 0 else (b[2], b[3])
            for half in range(nh):
                c0 = w0 + half * 512
                c1 = min(FF, c0 + 512)
                pairs = [(xvT[:, k * 128:(k + 1) * 128],
                          wk[:, k * FF + c0: k * FF + c1]) for k in range(8)]
                P.mm_acc(pj[half][:, 0:c1 - c0], pairs)
                P.s_act(relu_s[:, half * 512:half * 512 + c1 - c0],
                        pj[half][:, 0:c1 - c0], AF.Relu)
                P.s_act(kk_bf[:, c0 - w0:c1 - w0],
                        relu_s[:, half * 512:half * 512 + c1 - c0], AF.Square)
            for k in range(8 if wid == 1024 else 4):
                if w0 + k * 128 < FF:
                    P.transpose(pT[:, k * 128:(k + 1) * 128],
                                kk_bf[:, k * 128:(k + 1) * 128], cx.ident)
            P.v_copy(kkT[:, w0:w0 + wid], pT[:, 0:wid])
        proj_1024(P, (b[4], b[5]), xvTr[jm], wr)
        sig = sigs[jm]
        for half in range(2):
            P.s_act(sig[:, half * 512:(half + 1) * 512], b[4 + half], AF.Sigmoid)
        yb = (b[0], b[1]) if jm == 0 else (b[2], b[3])
        for half in range(2):
            pairs = [(kkT[:, kk2 * 128:(kk2 + 1) * 128],
                      wv[:, kk2 * D + half * 512: kk2 * D + half * 512 + 512])
                     for kk2 in range(28)]
            P.mm_acc(yb[half], pairs)





def round_floor_chain(P, cx, q, y_ap, out_bf, W, lo, hi):
    half = cx.cols[:, 3:4]
    neghalf = cx.cols[:, 4:5]
    P.s_act(q.i32[:, 0:W], y_ap, AF.Identity, bias=half)
    P.s_act(q.fm8[:, 0:W], q.i32[:, 0:W], AF.Identity, scale=1.0, bias=cx.cols[:, 6:7])
    P.v_tt(q.g[:, 0:W], q.fm8[:, 0:W], y_ap, ALU.is_gt)
    P.v_stt(q.y[:, 0:W], q.fm8[:, 0:W], 0.5, q.g[:, 0:W], ALU.add, ALU.subtract)
    P.v_ts(out_bf, q.y[:, 0:W], float(lo), ALU.max, float(hi), ALU.min)


def build_pass5(P, nc, cx, x_hbm, out_hbm, n_tiles, stats_in=None):
    HID = 4096
    ST = 256
    NS = n_tiles // 2
    A = lambda n, s, d: P.salloc(n, s, d)
    dram = lambda n, s: nc.dram_tensor(n, s, F32, kind="ExternalInput").ap()
    w1_h = dram("L1_ffn_w1", [HID, D])
    w2_h = dram("L1_ffn_w2", [D, HID])
    cw_h = dram("L1_ffn_conv_w", [HID, 1, 3])
    cb_h = dram("L1_ffn_conv_b", [HID])

    q = QS(P, "p5q")
    T1T = A("p5_T1T", [128, 8 * HID], BF16)
    T2T = A("p5_T2T", [128, 32 * D], BF16)
    _wsl = A("p5_wsl0", [128, D], F32)
    cwall = A("p5_cw", [128, 96], BF16)
    cball = A("p5_cb", [128, 32], BF16)
    acc = A("p5_acc", [128, 2], F32)
    swrow = A("p5_swrow", [1, 2], F32)
    mcols = A("p5_mcols", [128, 2], F32)
    minv = A("p5_minv", [128, 2], F32)
    onesf = A("p5_onesf", [1, 128], F32)
    _xi = A("p5_xin0", [128, D], F32)
    xin = [_xi, _xi]
    xout = [A(f"p5_xout{j}", [128, D], F32) for j in range(2)]
    wslab = [_wsl, xout[0]]
    stat = A("p5_stat", [128, 12], F32)
    scratch = q.y
    xn = q.fm8
    nx_bf = A("p5_nx", [128, D], BF16)
    wq_bf = nx_bf
    nxT = A("p5_nxT0", [128, 8 * ST], BF16)
    xcin = [q.i32.bitcast(F32), q.i32.bitcast(F32)]
    ivc = A("p5_ivc", [128, 2], F32)
    invs2 = A("p5_invs2", [128, 2], F32)
    h_bf = [A(f"p5_h{j}", [128, 32 * ST], BF16) for j in range(2)]
    hlast = A("p5_hlast", [128, 32], BF16)
    hfirst = A("p5_hfirst", [128, 32], BF16)
    qc = Ctx()
    qc.y = A("p5_qcy", [128, D], F32)
    qc.g = A("p5_qcg", [128, D], F32)
    qc.i32 = q.i32
    qc.fm8 = q.fm8
    tmpb = qc.g.bitcast(BF16)
    tmp2b = qc.y.bitcast(BF16)
    mt2 = _wsl[:, 0:ST]
    srep = _wsl[:, ST:2 * ST]
    rowtmp = _wsl[0:1, 2 * ST:3 * ST]
    invrep = _wsl[:, 3 * ST:4 * ST]

    PS = lambda n, s, d: P.palloc(n, s, d)
    b = [PS(f"p5_ps{i}", [128, 512], F32) for i in range(6)]
    pT = PS("p5_pT", [128, 1024], BF16)
    pT2 = PS("p5_pT2", [128, 1024], BF16)
    pTs = [pT, pT2]
    wqs = [nx_bf, tmpb[:, 0:D]]
    qs2 = [q, qc]

    eps = cx.cols[:, 0:1]
    P.v_memset(onesf, 1.0)
    P.v_memset(hfirst, 0.0)
    P.v_memset(hlast, 0.0)

    for wi, (wh, rows) in enumerate(((w1_h, HID), (w2_h, D))):
        nslab = rows // 128
        ac = acc[:, wi:wi + 1]
        P.v_memset(ac, 0.0)
        for s in range(nslab):
            for c4 in range(4 if wh is w2_h else 1):
                cidx = s * 4 + c4 if wh is w2_h else s
                sl = wslab[cidx % 2]
                if wh is w1_h:
                    P.dma("gpsimd", sl, wh.rearrange("(s p) n -> s p n", p=128)[s],
                          stream=f"w{cidx % 2}")
                else:
                    P.dma("gpsimd", sl,
                          wh.rearrange("(s p) (c n) -> s p c n", p=128, n=D)[s, :, c4],
                          stream=f"w{cidx % 2}")
                P.v_red(stat[:, cidx % 2:cidx % 2 + 1], sl, ALU.add, absval=True)
                P.v_tt(ac, ac, stat[:, cidx % 2:cidx % 2 + 1], ALU.add)
        P.mm(b[0][0:1, 0:1], ac, cx.cols[:, 2:3])
        P.v_ts(swrow[0:1, wi:wi + 1], b[0][0:1, 0:1], 1.0 / (HID * D), ALU.mult)
        P.v_ts(swrow[0:1, wi:wi + 1], swrow[0:1, wi:wi + 1], 1e-5, ALU.max)
        P.mm(b[1][:, 0:1], onesf, swrow[0:1, wi:wi + 1])
        P.v_copy(mcols[:, wi:wi + 1], b[1][:, 0:1])
    P.v_recip(minv, mcols)

    for wi, (wh, rows, dstT) in enumerate(((w1_h, HID, T1T), (w2_h, D, T2T))):
        nslab = rows // 128
        sw = minv[:, wi:wi + 1]
        for s in range(nslab):
            nchunk = 1 if wh is w1_h else 4
            for c4 in range(nchunk):
                cidx = s * nchunk + c4
                par = cidx % 2
                sl = wslab[par]
                qq = qs2[par]
                wq2 = wqs[par]
                pTt = pTs[par]
                if wh is w1_h:
                    P.dma("gpsimd", sl, wh.rearrange("(s p) n -> s p n", p=128)[s],
                          stream=f"w{par}")
                else:
                    P.dma("gpsimd", sl,
                          wh.rearrange("(s p) (c n) -> s p c n", p=128, n=D)[s, :, c4],
                          stream=f"w{par}")
                P.v_ts(qq.y[:, 0:D], sl, sw, ALU.mult)
                round_floor_chain(P, cx, qq, qq.y[:, 0:D], wq2, D, -1, 1)
                for k in range(8):
                    P.transpose(pTt[:, k * 128:(k + 1) * 128],
                                wq2[:, k * 128:(k + 1) * 128], cx.ident)
                if wh is w1_h:
                    dst = dstT.rearrange("p (k n) -> p k n", k=8)[:, :, 128 * s:128 * s + 128]
                else:
                    dst = dstT.rearrange("p (k n) -> p k n", k=32)[:, c4 * 8:(c4 + 1) * 8,
                                                                  128 * s:128 * s + 128]
                P.v_copy(dst, pTt.rearrange("p (k n) -> p k n", k=8))
    P.dma("gpsimd", cwall.rearrange("p (o s t) -> p o s t", o=1, s=32),
          cw_h.rearrange("(s p) o t -> p o s t", p=128), stream="w0",
          allow_slow_non_contiguous=True)
    P.dma("gpsimd", cball, cb_h.rearrange("(s p) -> p s", p=128), stream="w1",
          allow_slow_non_contiguous=True)

    rstd_all, negmr_all = ln_prepass(P, nc, cx, x_hbm, n_tiles, "p5pre",
                                     bufs=(xin[0], xin[1], q.g),
                                     stats_in=stats_in)
    x_t = x_hbm.rearrange("(n p) d -> n p d", p=128)
    o_t = out_hbm.rearrange("(n p) d -> n p d", p=128)
    cw3 = cwall.rearrange("p (s t) -> p s t", s=32)
    h3view = lambda hb: hb.rearrange("p (s t) -> p s t", t=ST)

    for s in range(NS + 1):
        js = s % 2
        if s < NS:
            for t2 in range(2):
                i = 2 * s + t2
                xi = xin[t2]
                P.dma("sync", xi, x_t[i], stream=f"xin{t2}")
                rstd, negmr = rstd_all[:, i:i + 1], negmr_all[:, i:i + 1]
                P.s_act(xn, xi, AF.Identity, bias=negmr, scale=rstd)
                mx, mxc, sc, iv = (stat[:, t:t + 1] for t in (5, 6, 7, 8))
                P.v_red(mx, xn, ALU.max, absval=True)
                P.v_ts(mxc, mx, 1e-5, ALU.max)
                P.v_recip(sc, mxc)
                P.v_ts(sc, sc, 127.0, ALU.mult)
                P.v_ts(q.y[:, 0:D], xn, sc, ALU.mult)
                round_floor_chain(P, cx, q, q.y[:, 0:D], nx_bf, D, -128, 127)
                P.v_tt(ivc[:, t2:t2 + 1], mxc, mcols[:, 0:1], ALU.mult)
                pTt = pT
                for k in range(8):
                    P.transpose(pTt[:, k * 128:(k + 1) * 128],
                                nx_bf[:, k * 128:(k + 1) * 128], cx.ident)
                P.v_copy(nxT.rearrange("p (k n) -> p k n", k=8)[:, :, t2 * 128:(t2 + 1) * 128],
                         pTt.rearrange("p (k n) -> p k n", k=8))
            P.v_ts(ivc, ivc, 1.0 / 127.0, ALU.mult)
            for t2 in range(2):
                P.mm(b[4][0:1, t2 * 128:(t2 + 1) * 128], ivc[:, t2:t2 + 1], cx.eyef)
            P.v_copy(rowtmp, b[4][0:1, 0:ST])
            P.mm(b[4][:, 0:ST], onesf, rowtmp)
            P.v_copy(invrep, b[4][:, 0:ST])
            hb = h_bf[js]
            for blk in range(32):
                bank = b[blk % 2]
                pairs = [(T1T[:, k * HID + blk * 128: k * HID + blk * 128 + 128],
                          nxT[:, k * ST:(k + 1) * ST]) for k in range(8)]
                P.mm_acc(bank[:, 0:ST], pairs)
                iv3 = invrep.rearrange("p (o t) -> p o t", o=1)
                P.v_tt(hb.rearrange("p (s t) -> p s t", t=ST)[:, blk:blk + 1, :],
                       bank[:, 0:ST].rearrange("p (o t) -> p o t", o=1), iv3, ALU.mult)
            P.v_copy(hfirst, h3view(hb)[:, :, 0:1].rearrange("p s o -> p (s o)"))
        if s == NS:
            P.v_memset(hfirst, 0.0)
        if s > 0:
            jp = (s - 1) % 2
            hp = h3view(h_bf[jp])
            ys = h_bf[jp]
            nh_bf = h_bf[jp]
            P.v_copy(hlast, hp[:, :, ST - 1:ST].rearrange("p s o -> p (s o)"))
            for c4 in range(4):
                sv = slice(c4 * 8, (c4 + 1) * 8)
                hpc = hp[:, sv, :]
                tp = tmpb.rearrange("p (s t) -> p s t", t=ST)[:, 0:8, :]
                tq = tmp2b.rearrange("p (s t) -> p s t", t=ST)[:, 0:8, :]
                cw0b = cw3[:, sv, 0:1].broadcast_to([128, 8, ST])
                cw1b = cw3[:, sv, 1:2].broadcast_to([128, 8, ST])
                cw2b = cw3[:, sv, 2:3].broadcast_to([128, 8, ST])
                cbb = cball[:, sv].rearrange("p (s o) -> p s o", o=1).broadcast_to([128, 8, ST])
                P.v_tt(tp[:, :, 0:ST - 1], hpc[:, :, 0:ST - 1], cw0b[:, :, 0:ST - 1], ALU.mult)
                P.v_tt(tq[:, :, 0:ST - 1], hpc[:, :, 1:ST], cw2b[:, :, 1:ST], ALU.mult)
                P.v_tt(hpc, hpc, cw1b, ALU.mult)
                P.v_tt(hpc, hpc, cbb, ALU.add)
                P.v_tt(hpc[:, :, 1:ST], hpc[:, :, 1:ST], tp[:, :, 0:ST - 1], ALU.add)
                P.v_tt(hpc[:, :, 0:ST - 1], hpc[:, :, 0:ST - 1], tq[:, :, 0:ST - 1], ALU.add)
            hl3 = hlast.rearrange("p (s o) -> p s o", o=1)
            hf3 = hfirst.rearrange("p (s o) -> p s o", o=1)
            tpz = tmpb[:, 0:32].rearrange("p (s o) -> p s o", o=1)
            P.v_tt(tpz, hl3, cw3[:, :, 0:1], ALU.mult)
            P.v_tt(hp[:, :, 0:1], hp[:, :, 0:1], tpz, ALU.add)
            P.v_tt(tpz, hf3, cw3[:, :, 2:3], ALU.mult)
            P.v_tt(hp[:, :, ST - 1:ST], hp[:, :, ST - 1:ST], tpz, ALU.add)
            P.s_act(ys, ys, AF.Silu)
            h3y = h3view(ys)
            tr0 = tmpb.rearrange("p (s t) -> p s t", t=ST)
            tr1 = tmp2b.rearrange("p (s t) -> p s t", t=ST)
            U16 = mybir.dt.uint16
            P.v_ts(tmpb.bitcast(U16), ys[:, 0:8 * ST].bitcast(U16), 0x7FFF,
                   ALU.bitwise_and)
            P.v_ts(tmp2b.bitcast(U16), ys[:, 8 * ST:16 * ST].bitcast(U16), 0x7FFF,
                   ALU.bitwise_and)
            P.v_tt(tr0, tr0, tr1, ALU.max)
            P.v_ts(tmp2b.bitcast(U16), ys[:, 16 * ST:24 * ST].bitcast(U16), 0x7FFF,
                   ALU.bitwise_and)
            P.v_tt(tr0, tr0, tr1, ALU.max)
            P.v_ts(tmp2b.bitcast(U16), ys[:, 24 * ST:32 * ST].bitcast(U16), 0x7FFF,
                   ALU.bitwise_and)
            P.v_tt(tr0, tr0, tr1, ALU.max)
            P.v_tt(tr0[:, 0:4, :], tr0[:, 0:4, :], tr0[:, 4:8, :], ALU.max)
            P.v_tt(tr0[:, 0:2, :], tr0[:, 0:2, :], tr0[:, 2:4, :], ALU.max)
            P.v_tt(nx_bf[:, 0:ST].rearrange("p (o t) -> p o t", o=1),
                   tr0[:, 0:1, :], tr0[:, 1:2, :], ALU.max)
            for t2 in range(2):
                P.transpose(pT[:, t2 * 128:(t2 + 1) * 128],
                            nx_bf[:, t2 * 128:(t2 + 1) * 128], cx.ident)
            P.v_copy(mt2, pT[:, 0:ST])
            mxc2 = stat[:, 9:11]
            P.v_red(mxc2.rearrange("p (t o) -> p t o", o=1),
                    mt2.rearrange("p (t l) -> p t l", l=128), ALU.max)
            P.v_ts(mxc2, mxc2, 1e-5, ALU.max)
            P.v_recip(invs2, mxc2)
            P.v_ts(invs2, invs2, 127.0, ALU.mult)
            for t2 in range(2):
                P.mm(b[4][0:1, t2 * 128:(t2 + 1) * 128], invs2[:, t2:t2 + 1], cx.eyef)
            P.v_copy(rowtmp, b[4][0:1, 0:ST])
            P.mm(b[4][:, 0:ST], onesf, rowtmp)
            P.v_copy(srep, b[4][:, 0:ST])
            P.v_tt(invs2, mxc2, mcols[:, 1:2].broadcast_to([128, 2]), ALU.mult)
            P.v_ts(invs2, invs2, 1.0 / 127.0, ALU.mult)
            for c4 in range(8):
                sl4 = slice(c4 * D, (c4 + 1) * D)
                sr3 = srep.rearrange("p (o t) -> p o t", o=1).broadcast_to([128, 4, ST])
                P.v_tt(qc.y[:, 0:D].rearrange("p (s t) -> p s t", t=ST),
                       ys[:, sl4].rearrange("p (s t) -> p s t", t=ST), sr3, ALU.mult)
                round_floor_chain(P, cx, qc, qc.y[:, 0:D], nh_bf[:, sl4], D, -128, 127)
            nh3 = hp
            for t2 in range(2):
                i = 2 * (s - 1) + t2
                yb = (b[2], b[3]) if t2 == 0 else (b[4], b[5])
                for half in range(2):
                    pairs = [(nh3[:, sl5, t2 * 128:(t2 + 1) * 128],
                              T2T[:, sl5 * D + half * 512: sl5 * D + half * 512 + 512])
                             for sl5 in range(32)]
                    P.mm_acc(yb[half], pairs)
                xc = xcin[t2]
                P.dma("sync", xc, x_t[i], stream=f"xc{t2}")
                xo = xout[t2]
                for half in range(2):
                    sl6 = slice(half * 512, (half + 1) * 512)
                    P.v_stt(xo[:, sl6], yb[half], invs2[:, t2:t2 + 1], xc[:, sl6],
                            ALU.mult, ALU.add)
                P.dma("sync", o_t[i], xo, stream=f"xo{t2}")



def build_nc(passes, n_tiles=SEQ // 128, p4_x3_external=False):
    nc = bass.Bass("TRN2", target_bir_lowering=False)
    P = Prog(nc)
    cx = setup_consts(P, nc)
    x_in = nc.dram_tensor("L0_attn_x", [n_tiles * 128, D], F32, kind="ExternalInput").ap()
    bufs = {"x": x_in}
    for nm in ("x1", "x2", "xm", "x4"):
        bufs[nm] = nc.dram_tensor("stage_" + nm, [n_tiles * 128, D], F32, kind="Internal").ap()
    out = nc.dram_tensor("out", [n_tiles * 128, D], F32, kind="ExternalOutput").ap()
    bufs["out"] = out
    io = {"p1": ("x", "x1"), "p2": ("x1", "x2"), "p3": ("x2", "xm"),
          "p4": ("xm", "x4"), "p5": ("x4", "out")}
    last = passes[-1]
    builders = {"p1": build_pass1, "p2": build_pass2, "p3": build_pass3,
                "p4": build_pass4, "p5": build_pass5}
    chain_stats = {}
    shared_pairs = []
    for a, bname in zip(passes, passes[1:]):
        if io[a][1] == io[bname][0]:
            k = len(chain_stats) % 2
            if k >= len(shared_pairs):
                shared_pairs.append(
                    (P.salloc(f"hs{k}_sums", [128, n_tiles], F32),
                     P.salloc(f"hs{k}_ssq", [128, n_tiles], F32)))
            chain_stats[bname] = shared_pairs[k]
    for idx, p in enumerate(passes):
        src, dst = io[p]
        if p == passes[0]:
            src = "x"
        if p == last:
            dst = "out"
        kw = {}
        if p in chain_stats:
            kw["stats_in"] = chain_stats[p]
        if idx + 1 < len(passes) and passes[idx + 1] in chain_stats:
            kw["stats_out"] = chain_stats[passes[idx + 1]]
        P.guards = []
        if p == "p4":
            if p4_x3_external:
                x3b = nc.dram_tensor("x3_in", [n_tiles * 128, D], F32,
                                     kind="ExternalInput").ap()
            elif passes[0] == "p4":
                x3b = bufs["x"]
            else:
                x3b = bufs["x2"]
            builders[p](P, nc, cx, bufs[src], x3b, bufs[dst], n_tiles, **kw)
        else:
            builders[p](P, nc, cx, bufs[src], bufs[dst], n_tiles, **kw)
        P.barrier()
        for g in reversed(P.guards):
            g.__exit__(None, None, None)
        P.guards = None
    P.final_waits()
    P.finalize()
    return nc


def _declared_inputs(nc):
    names = []
    for a in nc.m.functions[0].allocations:
        if isinstance(a, mybir.MemoryLocationSet) and a.kind == "ExternalInput":
            names.append(a.memorylocations[0].name)
    return set(names)


_NC_CACHE = {}


def _launch(passes, per_core_x, all_inputs, consts, extra=None):
    B = len(per_core_x)
    T = per_core_x[0].shape[0]
    key = (tuple(passes), T, extra is not None)
    if key not in _NC_CACHE:
        _NC_CACHE[key] = build_nc(list(passes), n_tiles=T // 128,
                                  p4_x3_external=extra is not None)
    nc = _NC_CACHE[key]
    declared = _declared_inputs(nc)
    in_maps = []
    for c in range(B):
        m = {}
        for nm in declared:
            if nm == "partition_id":
                continue
            if nm == "L0_attn_x":
                m[nm] = per_core_x[c]
            elif nm == "x3_in":
                m[nm] = extra[c]
            elif nm in consts:
                m[nm] = consts[nm]
            else:
                m[nm] = np.ascontiguousarray(np.asarray(all_inputs[nm], np.float32))
        in_maps.append(m)
    res = run_bass_kernel_spmd(nc, in_maps, core_ids=list(range(B)))
    return [r["out"] for r in res.results]


def kernel(**inputs):
    x = np.ascontiguousarray(np.asarray(inputs["L0_attn_x"], dtype=np.float32))
    B, T, d = x.shape
    consts = const_arrays()
    cur = [x[c] for c in range(B)]
    out = _launch(["p1", "p2", "p3", "p4", "p5"], cur, inputs, consts)
    return np.stack(out, axis=0)


# version 25
# speedup vs baseline: 1.0071x; 1.0071x over previous
import numpy as np
import ml_dtypes
import concourse.bass as bass
import concourse.mybir as mybir
from concourse.bass_utils import run_bass_kernel_spmd

F32 = mybir.dt.float32
BF16 = mybir.dt.bfloat16
I32 = mybir.dt.int32
U32 = mybir.dt.uint32
AF = mybir.ActivationFunctionType
ALU = mybir.AluOpType
AX = mybir.AxisListType

D = 1024
NCORES = 8
SEQ = 4096

ENGINES = ("sync", "vector", "scalar", "tensor", "gpsimd")


class Prog:

    def __init__(self, nc):
        self.nc = nc
        self.ops = {e: [] for e in ENGINES}
        self.sems = {}
        self.val = {}
        for e in ENGINES:
            self.sems[e] = nc.semaphore(f"sem_{e}").__enter__()
            self.val[e] = 0
        self.dma_streams = {}
        self.last_w = {}
        self.readers = {}
        self.waited = {e: {} for e in ENGINES}
        self.guards = None

    @staticmethod
    def _name(ap):
        t = getattr(ap, "tensor", ap)
        return t.name

    @staticmethod
    def _region(ap):
        t = getattr(ap, "tensor", None)
        if t is None:
            return None
        try:
            shape = list(t.shape)
            if len(shape) != 2:
                return None
            rowlen = shape[1]
            off = ap.offset
            dims = [list(d) for d in ap.ap]
            if not dims:
                return None
            p_step, p_cnt = dims[0]
            if p_step == rowlen or shape[0] == 1 or p_cnt == 1:
                p0 = off // rowlen
                f0 = off % rowlen
                fext = 1
                for st, cnt in dims[1:]:
                    fext += (cnt - 1) * abs(st)
                return (p0, p0 + p_cnt, f0, f0 + fext)
        except Exception:
            pass
        return None

    @staticmethod
    def _overlap(a, b):
        if a is None or b is None:
            return True
        return a[0] < b[1] and b[0] < a[1] and a[2] < b[3] and b[2] < a[3]

    def _space(self, ap):
        t = getattr(ap, "tensor", None)
        n = type(t).__name__ if t is not None else ""
        if "PSum" in n:
            return "psum"
        if "SB" in n:
            return "sbuf"
        return "dram"

    def _wait(self, eng, semkey, v):
        if semkey.startswith("d_"):
            v = self.val[semkey]
        if v <= 0:
            return
        w = self.waited[eng]
        if w.get(semkey, 0) >= v:
            return
        w[semkey] = v
        sem = self.sems[semkey]
        self.ops[eng].append(lambda E, sem=sem, v=v: E.wait_ge(sem, v))

    def _entries(self, table, name):
        return table.setdefault(name, [])

    def _compact(self, lst):
        if len(lst) > 48:
            best = {}
            for semkey, v, reg, eng in lst:
                cur = best.get(semkey)
                if cur is None or v > cur[1]:
                    best[semkey] = (semkey, v, None, eng)
            lst[:] = list(best.values())

    def _pre(self, eng, reads, writes):
        for ap in reads:
            n = self._name(ap)
            reg = self._region(ap)
            full_bank = self._space(ap) == "psum"
            for semkey, v, wreg, weng in self._entries(self.last_w, n):
                creg = None if (full_bank and weng != eng) else reg
                wreg2 = None if (full_bank and weng != eng) else wreg
                if self._overlap(creg, wreg2):
                    self._wait(eng, semkey, v)
        for ap in writes:
            n = self._name(ap)
            reg = self._region(ap)
            full_bank = self._space(ap) == "psum"
            for semkey, v, wreg, weng in self._entries(self.last_w, n):
                creg = None if (full_bank and weng != eng) else reg
                wreg2 = None if (full_bank and weng != eng) else wreg
                if self._overlap(creg, wreg2):
                    self._wait(eng, semkey, v)
            for semkey, v, rreg, reng in self._entries(self.readers, n):
                creg = None if (full_bank and reng != eng) else reg
                rreg2 = None if (full_bank and reng != eng) else rreg
                if self._overlap(creg, rreg2):
                    self._wait(eng, semkey, v)

    def _post(self, semkey, newval, reads, writes, eng=None):
        for ap in reads:
            n = self._name(ap)
            lst = self._entries(self.readers, n)
            lst.append((semkey, newval, self._region(ap), eng))
            self._compact(lst)
        for ap in writes:
            n = self._name(ap)
            reg = self._region(ap)
            wl = self._entries(self.last_w, n)
            if reg is None:
                wl[:] = [(semkey, newval, None, eng)]
                self.readers[n] = []
            else:
                wl[:] = [e for e in wl if not (e[2] is not None and
                                               reg[0] <= e[2][0] and e[2][1] <= reg[1] and
                                               reg[2] <= e[2][2] and e[2][3] <= reg[3])]
                wl.append((semkey, newval, reg, eng))
                self._compact(wl)
                rl = self._entries(self.readers, n)
                rl[:] = [e for e in rl if not (e[2] is not None and
                                               reg[0] <= e[2][0] and e[2][1] <= reg[1] and
                                               reg[2] <= e[2][2] and e[2][3] <= reg[3])]

    def op(self, eng, fn, reads=(), writes=()):
        self._pre(eng, reads, writes)
        self.val[eng] += 1
        newval = self.val[eng]
        sem = self.sems[eng]

        def emit(E, fn=fn, sem=sem):
            fn(E).then_inc(sem, 1)
        self.ops[eng].append(emit)
        self._post(eng, newval, reads, writes, eng=eng)

    def dma(self, queue, out, in_, stream=None, **kw):
        semkey = "d_" + (stream if stream is not None else queue)
        if semkey not in self.sems:
            self.sems[semkey] = self.nc.semaphore(semkey).__enter__()
            self.val[semkey] = 0
            self.dma_streams[semkey] = queue
        assert self.dma_streams[semkey] == queue, (semkey, queue)
        self._wait(queue, semkey, self.val[semkey])
        self._pre(queue, [in_], [out])
        self.val[semkey] += 16
        newval = self.val[semkey]
        sem = self.sems[semkey]

        def emit(E, out=out, in_=in_, kw=kw, sem=sem):
            E.dma_start(out, in_, **kw).then_inc(sem, 16)
        self.ops[queue].append(emit)
        self._post(semkey, newval, [in_], [out], eng=queue)

    def raw(self, eng, fn):
        self.ops[eng].append(fn)

    def salloc(self, name, shape, dtype):
        g = self.nc.sbuf_tensor(name, shape, dtype)
        h = g.__enter__()
        if self.guards is not None:
            self.guards.append(g)
        return h[:]

    def palloc(self, name, shape, dtype):
        g = self.nc.psum_tensor(name, shape, dtype)
        h = g.__enter__()
        if self.guards is not None:
            self.guards.append(g)
        return h[:]

    def barrier(self):
        for eng in ENGINES:
            for semkey, v in self.val.items():
                self._wait(eng, semkey, v)

    def final_waits(self):
        for semkey, q in self.dma_streams.items():
            v = self.val[semkey]
            if v > 0:
                sem = self.sems[semkey]
                self.ops[q].append(lambda E, sem=sem, v=v: E.wait_ge(sem, v))

    def finalize(self):
        nc = self.nc
        block = nc.Block().__enter__()
        for ename in ENGINES:
            oplist = self.ops[ename]
            if not oplist:
                continue
            deco = getattr(block, ename)

            def body(E, oplist=oplist):
                for f in oplist:
                    f(E)
            deco(body)
        block.__exit__(None, None, None)

    def v_tt(self, out, a, b, alu):
        self.op("vector", lambda E: E.tensor_tensor(out, a, b, alu),
                reads=[a, b], writes=[out])

    def v_ts(self, out, a, s1, alu, s2=None, alu2=None):
        r = [a]
        if hasattr(s1, "tensor"):
            r.append(s1)
        if s2 is not None and hasattr(s2, "tensor"):
            r.append(s2)
        if s2 is None and alu2 is None:
            self.op("vector", lambda E: E.tensor_single_scalar(out, a, s1, alu),
                    reads=r, writes=[out])
        else:
            self.op("vector",
                    lambda E: E.tensor_scalar(out, a, s1, s2, alu, alu2),
                    reads=r, writes=[out])

    def v_copy(self, out, a):
        self.op("vector", lambda E: E.tensor_copy(out, a), reads=[a], writes=[out])

    def v_red(self, out, a, alu, absval=False):
        self.op("vector",
                lambda E: E.tensor_reduce(out, a, AX.X, alu,
                                          apply_absolute_value=absval),
                reads=[a], writes=[out])

    def v_stt(self, out, in0, scal, in1, alu0, alu1):
        r = [in0, in1]
        if hasattr(scal, "tensor"):
            r.append(scal)
        self.op("vector",
                lambda E: E.scalar_tensor_tensor(out, in0, scal, in1, alu0, alu1),
                reads=r, writes=[out])

    def v_scan(self, out, d0, d1, init, alu0, alu1):
        r = [d0, d1]
        if hasattr(init, "tensor"):
            r.append(init)
        self.op("vector",
                lambda E: E.tensor_tensor_scan(out, d0, d1, init, alu0, alu1),
                reads=r, writes=[out])

    def v_recip(self, out, a):
        self.op("vector", lambda E: E.reciprocal(out, a), reads=[a], writes=[out])

    def v_memset(self, out, c):
        self.op("vector", lambda E: E.memset(out, c), reads=[], writes=[out])

    def g_memset(self, out, c):
        self.op("gpsimd", lambda E: E.memset(out, c), reads=[], writes=[out])

    def g_tt(self, out, a, b, alu):
        self.op("gpsimd", lambda E: E.tensor_tensor(out, a, b, alu),
                reads=[a, b], writes=[out])

    def s_act(self, out, a, func, bias=0.0, scale=1.0, accum=None):
        r = [a]
        if hasattr(bias, "tensor"):
            r.append(bias)
        if hasattr(scale, "tensor"):
            r.append(scale)
        w = [out] + ([accum] if accum is not None else [])
        if accum is None:
            self.op("scalar",
                    lambda E: E.activation(out, a, func, bias=bias, scale=scale),
                    reads=r, writes=w)
        else:
            self.op("scalar",
                    lambda E: E.activation(out, a, func, bias=bias, scale=scale,
                                           accum_out=accum),
                    reads=r, writes=w)

    def s_copy(self, out, a):
        self.op("scalar", lambda E: E.copy(out, a), reads=[a], writes=[out])

    def mm_acc(self, out, pairs):
        n = len(pairs)
        allreads = []
        for (l, r) in pairs:
            allreads.append(l)
            allreads.append(r)
        self._pre("tensor", allreads, [out])
        for i, (l, r) in enumerate(pairs[:-1]):
            self.ops["tensor"].append(
                lambda E, out=out, l=l, r=r, st=(i == 0):
                E.matmul(out, l, r, start=st, stop=False))
        self.val["tensor"] += 1
        newval = self.val["tensor"]
        sem = self.sems["tensor"]
        l, r = pairs[-1]

        def emit(E, out=out, l=l, r=r, st=(n == 1), sem=sem):
            E.matmul(out, l, r, start=st, stop=True).then_inc(sem, 1)
        self.ops["tensor"].append(emit)
        self._post("tensor", newval, allreads, [out], eng="tensor")

    def mm(self, out, lhsT, rhs):
        self.mm_acc(out, [(lhsT, rhs)])

    def mm_part(self, out, lhsT, rhs, start, stop, final):
        self._pre("tensor", [lhsT, rhs], [out] if start else [])
        if final:
            self.val["tensor"] += 1
            newval = self.val["tensor"]
            sem = self.sems["tensor"]

            def emit(E, out=out, l=lhsT, r=rhs, start=start, stop=stop, sem=sem):
                E.matmul(out, l, r, start=start, stop=stop).then_inc(sem, 1)
            self.ops["tensor"].append(emit)
            self._post("tensor", newval, [lhsT, rhs], [out], eng="tensor")
        else:
            self.ops["tensor"].append(
                lambda E, out=out, l=lhsT, r=rhs, start=start, stop=stop:
                E.matmul(out, l, r, start=start, stop=stop))

    def transpose(self, out_ps, in_sb, ident):
        self._pre("tensor", [in_sb, ident], [out_ps])
        self.val["tensor"] += 1
        newval = self.val["tensor"]
        sem = self.sems["tensor"]

        def emit(E, out=out_ps, a=in_sb, i=ident, sem=sem):
            E.transpose(out, a, i).then_inc(sem, 1)
        self.ops["tensor"].append(emit)
        self._post("tensor", newval, [in_sb, ident], [out_ps], eng="tensor")



def const_arrays():
    c = {}
    c["zz_ident"] = np.eye(128, dtype=ml_dtypes.bfloat16)
    m = np.zeros((128, 128), np.float32)
    for c0 in (0, 64):
        for s in range(64):
            m[c0 + s, c0 + s:c0 + 64] = 1.0
    c["zz_cum64"] = m
    c["zz_cum128"] = np.fromfunction(
        lambda s, t: (s <= t).astype(np.float32), (128, 128)).astype(np.float32)
    c["zz_mask_incl64"] = np.fromfunction(
        lambda s, t: (s <= t).astype(np.float32), (64, 64)).astype(np.float32)
    c["zz_mask_strict128"] = np.fromfunction(
        lambda s, t: (s < t).astype(np.float32), (128, 128)).astype(np.float32)
    oh = np.zeros((128, 2), ml_dtypes.bfloat16)
    oh[63, 0] = 1
    oh[127, 1] = 1
    c["zz_onehot"] = oh
    c["zz_eyef"] = np.eye(128, dtype=np.float32)
    return c


class Ctx:
    pass


def setup_consts(P, nc):
    cx = Ctx()
    names = {"zz_ident": ([128, 128], BF16), "zz_cum64": ([128, 128], F32),
             "zz_cum128": ([128, 128], F32), "zz_mask_incl64": ([64, 64], F32),
             "zz_mask_strict128": ([128, 128], F32), "zz_onehot": ([128, 2], BF16),
             "zz_eyef": ([128, 128], F32)}
    for n, (shape, dt) in names.items():
        h = nc.dram_tensor(n, shape, dt, kind="ExternalInput").ap()
        sb = nc.alloc_sbuf_tensor("c_" + n[3:], shape, dt)[:]
        P.dma("sync", sb, h, stream="consts")
        setattr(cx, n[3:], sb)
    cx.cols = nc.alloc_sbuf_tensor("c_cols", [128, 8], F32)[:]
    P.v_memset(cx.cols[:, 0:1], 1e-5)
    P.v_memset(cx.cols[:, 1:2], 1e-24)
    P.v_memset(cx.cols[:, 2:3], 1.0)
    P.v_memset(cx.cols[:, 3:4], 0.5)
    P.v_memset(cx.cols[:, 4:5], -0.0625)
    P.v_memset(cx.cols[:, 5:6], 0.0)
    P.v_memset(cx.cols[:, 6:7], -0.5)
    P.v_memset(cx.cols[:, 7:8], 0.0)
    cx.ones_bf = nc.alloc_sbuf_tensor("c_onesbf", [1, 128], BF16)[:]
    P.v_memset(cx.ones_bf, 1.0)
    cx.kcol = nc.alloc_sbuf_tensor("c_kcol", [128, 1], I32)[:]
    P.v_memset(cx.kcol, 0x5F3759DF)
    return cx


def load_weight_bf16(P, nc, name, hbm_ap, rows, cols):
    nk = rows // 128
    sb = P.salloc(name, [128, nk * cols], BF16)
    src = hbm_ap.rearrange("(k p) n -> p k n", p=128)
    dst = sb.rearrange("p (k n) -> p k n", k=nk)
    P.dma("gpsimd", dst, src, stream="weights")
    return sb


def ln_stats(P, x_in, scratch, stat, eps_col):
    ss, sums, mcol, rstd, negmr = (stat[:, i:i + 1] for i in range(5))
    P.s_act(scratch, x_in, AF.Square, accum=ss)
    P.v_red(sums, x_in, ALU.add)
    P.v_ts(mcol, sums, 1.0 / D, ALU.mult)
    P.v_ts(rstd, ss, 1.0 / D, ALU.mult)
    P.v_tt(negmr, mcol, mcol, ALU.mult)
    P.v_tt(rstd, rstd, negmr, ALU.subtract)
    P.s_act(rstd, rstd, AF.Ln, bias=eps_col)
    P.s_act(rstd, rstd, AF.Exp, scale=-0.5)
    P.v_tt(negmr, mcol, rstd, ALU.mult)
    P.v_ts(negmr, negmr, -1.0, ALU.mult)
    return rstd, negmr


def transpose_1024(P, cx, src_bf, dst_ps, dst_sb, evac="vector", nblk=8):
    for k in range(nblk):
        P.transpose(dst_ps[:, k * 128:(k + 1) * 128],
                    src_bf[:, k * 128:(k + 1) * 128], cx.ident)
    if evac == "vector":
        P.v_copy(dst_sb[:, 0:nblk * 128], dst_ps[:, 0:nblk * 128])
    else:
        P.s_copy(dst_sb[:, 0:nblk * 128], dst_ps[:, 0:nblk * 128])


def proj_1024(P, out_ps_pair, xT, w_sb, nk=8, ncols=1024):
    nh = (ncols + 511) // 512
    for half in range(nh):
        w0 = half * 512
        w1 = min(ncols, w0 + 512)
        pairs = []
        for k in range(nk):
            lhsT = xT[:, k * 128:(k + 1) * 128]
            rhs = w_sb[:, k * ncols + w0: k * ncols + w1]
            pairs.append((lhsT, rhs))
        P.mm_acc(out_ps_pair[half][:, 0:w1 - w0], pairs)


def ln_prepass(P, nc, cx, x_hbm, n_tiles, tag, width=D, bufs=None,
               stats_in=None):
    A = lambda n, s, d: P.salloc(n, s, d)
    rstd = A(f"{tag}_rstd", [128, n_tiles], F32)
    negmr = A(f"{tag}_negmr", [128, n_tiles], F32)
    outer_guards = P.guards
    P.guards = []
    if stats_in is not None:
        sums, ssq = stats_in
        t1 = bufs[2][:, 0:n_tiles]
        t2 = bufs[2][:, n_tiles:2 * n_tiles]
    else:
        t1 = A(f"{tag}_t1", [128, n_tiles], F32)
        t2 = A(f"{tag}_t2", [128, n_tiles], F32)
        sums = A(f"{tag}_sums", [128, n_tiles], F32)
        ssq = A(f"{tag}_ssq", [128, n_tiles], F32)
        if bufs is None:
            xb = [A(f"{tag}_px{j}", [128, width], F32) for j in range(2)]
            scr = A(f"{tag}_pscr", [128, width], F32)
        else:
            xb = [bufs[0], bufs[1]]
            scr = bufs[2]
        x_t = x_hbm.rearrange("(n p) d -> n p d", p=128)
        for i in range(n_tiles):
            xi = xb[i % 2]
            P.dma("sync", xi, x_t[i], stream=f"pre{i % 2}")
            P.s_act(scr, xi, AF.Square, accum=ssq[:, i:i + 1])
            P.v_red(sums[:, i:i + 1], xi, ALU.add)
    eps = cx.cols[:, 0:1]
    P.v_ts(sums, sums, 1.0 / width, ALU.mult)
    P.v_ts(ssq, ssq, 1.0 / width, ALU.mult)
    P.v_tt(t1, sums, sums, ALU.mult)
    P.v_tt(ssq, ssq, t1, ALU.subtract)
    P.v_ts(ssq, ssq, eps, ALU.add)
    rsqrt_col(P, cx, rstd, ssq, t1, t2)
    P.v_tt(negmr, sums, rstd, ALU.mult)
    P.v_ts(negmr, negmr, -1.0, ALU.mult)
    P.barrier()
    for g in reversed(P.guards):
        g.__exit__(None, None, None)
    P.guards = outer_guards
    return rstd, negmr


def ln_prepass_double(P, nc, cx, x_hbm, n_tiles, tag, bufs=None,
                      stats_in=None):
    A = lambda n, s, d: P.salloc(n, s, d)
    r1 = A(f"{tag}_r1", [128, n_tiles], F32)
    f2o = A(f"{tag}_f2", [128, n_tiles], F32)
    outer_guards = P.guards
    P.guards = []
    if stats_in is not None:
        sums, ssq = stats_in
        s2 = bufs[2][:, 0:n_tiles]
        t1 = bufs[2][:, n_tiles:2 * n_tiles]
        t2 = bufs[2][:, 2 * n_tiles:3 * n_tiles]
    else:
        s2 = A(f"{tag}_s2", [128, n_tiles], F32)
        t1 = A(f"{tag}_t1", [128, n_tiles], F32)
        t2 = A(f"{tag}_t2", [128, n_tiles], F32)
        sums = A(f"{tag}_sums", [128, n_tiles], F32)
        ssq = A(f"{tag}_ssq", [128, n_tiles], F32)
        if bufs is None:
            xb = [A(f"{tag}_px{j}", [128, D], F32) for j in range(2)]
            scr = A(f"{tag}_pscr", [128, D], F32)
        else:
            xb = [bufs[0], bufs[1]]
            scr = bufs[2]
        x_t = x_hbm.rearrange("(n p) d -> n p d", p=128)
        for i in range(n_tiles):
            xi = xb[i % 2]
            P.dma("sync", xi, x_t[i], stream=f"pre{i % 2}")
            P.s_act(scr, xi, AF.Square, accum=ssq[:, i:i + 1])
            P.v_red(sums[:, i:i + 1], xi, ALU.add)
    eps = cx.cols[:, 0:1]
    epsb = eps
    P.v_ts(sums, sums, 1.0 / D, ALU.mult)
    P.v_ts(ssq, ssq, 1.0 / D, ALU.mult)
    P.v_tt(t1, sums, sums, ALU.mult)
    P.v_tt(ssq, ssq, t1, ALU.subtract)
    P.v_ts(ssq, ssq, epsb, ALU.add)
    rsqrt_col(P, cx, r1, ssq, t1, t2)
    P.v_recip(t1, ssq)
    P.v_ts(ssq, ssq, epsb, ALU.subtract)
    P.v_tt(t1, ssq, t1, ALU.mult)
    P.v_ts(t1, t1, epsb, ALU.add)
    rsqrt_col(P, cx, s2, t1, t2, ssq)
    P.v_tt(r1, r1, s2, ALU.mult)
    P.v_tt(f2o, sums, r1, ALU.mult)
    P.v_ts(f2o, f2o, -1.0, ALU.mult)
    P.barrier()
    for g in reversed(P.guards):
        g.__exit__(None, None, None)
    P.guards = outer_guards
    return r1, f2o


def emit_out_stats(P, stats_out, xo, dump, i):
    sums, ssq = stats_out
    P.s_act(dump, xo, AF.Square, accum=ssq[:, i:i + 1])
    P.v_red(sums[:, i:i + 1], xo, ALU.add)


def rsqrt_col(P, cx, out, v, s1, s2):
    W = v.shape[-1]
    P.v_ts(s1.bitcast(I32), v.bitcast(I32), 1, ALU.logical_shift_right)
    P.v_tt(out.bitcast(I32), cx.kcol.broadcast_to([128, W]), s1.bitcast(I32),
           ALU.subtract)
    for _ in range(3):
        P.v_tt(s1, out, out, ALU.mult)
        P.v_tt(s2, s1, v, ALU.mult)
        P.v_ts(s2, s2, -0.5, ALU.mult, 1.5, ALU.add)
        P.v_tt(out, out, s2, ALU.mult)


def ln_stats_nolut(P, cx, x_in, scratch, stat, eps_col, width=D):
    ss, sums, mcol, rstd, negmr, t1, t2, var = (stat[:, i:i + 1] for i in range(8))
    P.s_act(scratch[:, 0:width], x_in, AF.Square, accum=ss)
    P.v_red(sums, x_in, ALU.add)
    P.v_ts(mcol, sums, 1.0 / width, ALU.mult)
    P.v_ts(var, ss, 1.0 / width, ALU.mult)
    P.v_tt(t1, mcol, mcol, ALU.mult)
    P.v_tt(var, var, t1, ALU.subtract)
    P.v_ts(var, var, eps_col, ALU.add)
    rsqrt_col(P, cx, rstd, var, t1, t2)
    P.v_tt(negmr, mcol, rstd, ALU.mult)
    P.v_ts(negmr, negmr, -1.0, ALU.mult)
    return rstd, negmr


class QS:

    def __init__(self, P, tag, W=1024):
        A = lambda n, s, d: P.salloc(n, s, d)
        self.y = A(f"{tag}_y", [128, W], F32)
        self.g = A(f"{tag}_g", [128, W], F32)
        self.i32 = A(f"{tag}_i32", [128, W], I32)
        self.fm8 = self.i32.bitcast(F32)
        self.am = A(f"{tag}_am", [128, W // 32], F32)
        self.inv = A(f"{tag}_inv", [128, W // 32], F32)
        self.sc = A(f"{tag}_sc", [128, W // 32], F32)


def quant_chain(P, cx, q, src, dst_bf, W=1024, gt_eng="vector",
                y_eng="vector"):
    nb = W // 32
    src3 = src.rearrange("p (b k) -> p b k", k=32)
    am3 = q.am[:, 0:nb].rearrange("p (b o) -> p b o", o=1)
    P.v_red(am3, src3, ALU.max, absval=True)
    P.v_ts(q.am[:, 0:nb], q.am[:, 0:nb], 1.1754944e-38, ALU.max)
    P.v_ts(q.am[:, 0:nb].bitcast(U32), q.am[:, 0:nb].bitcast(U32),
           0x7F800000, ALU.bitwise_and)
    P.v_recip(q.inv[:, 0:nb], q.am[:, 0:nb])
    P.v_ts(q.sc[:, 0:nb], q.am[:, 0:nb], 0.125, ALU.mult)
    inv_bc = q.inv[:, 0:nb].rearrange("p (b o) -> p b o", o=1).broadcast_to([128, nb, 32])
    sc_bc = q.sc[:, 0:nb].rearrange("p (b o) -> p b o", o=1).broadcast_to([128, nb, 32])
    y3 = q.y[:, 0:W].rearrange("p (b k) -> p b k", k=32)
    if y_eng == "gpsimd":
        P.g_tt(y3, src3, inv_bc, ALU.mult)
    else:
        P.v_tt(y3, src3, inv_bc, ALU.mult)
    P.s_act(q.i32[:, 0:W], q.y[:, 0:W], AF.Identity, scale=8.0, bias=cx.cols[:, 3:4])
    P.s_act(q.fm8[:, 0:W], q.i32[:, 0:W], AF.Identity, scale=0.125, bias=cx.cols[:, 4:5])
    if gt_eng == "gpsimd":
        P.g_tt(q.g[:, 0:W], q.fm8[:, 0:W], q.y[:, 0:W], ALU.is_gt)
    else:
        P.v_tt(q.g[:, 0:W], q.fm8[:, 0:W], q.y[:, 0:W], ALU.is_gt)
    P.v_stt(q.y[:, 0:W], q.fm8[:, 0:W], 8.0, q.g[:, 0:W], ALU.mult, ALU.subtract)
    dst3 = dst_bf.rearrange("p (b k) -> p b k", k=32)
    P.v_stt(dst3, y3, 0.5, sc_bc, ALU.add, ALU.mult)



def build_pass2(P, nc, cx, x_hbm, out_hbm, n_tiles, stats_in=None,
                stats_out=None):
    HID = 4096
    wfc_h = nc.dram_tensor("L0_ffn_w_fc", [HID, D], F32, kind="ExternalInput").ap()
    bfc_h = nc.dram_tensor("L0_ffn_b_fc", [HID], F32, kind="ExternalInput").ap()
    wpr_h = nc.dram_tensor("L0_ffn_w_proj", [D, HID], F32, kind="ExternalInput").ap()
    bpr_h = nc.dram_tensor("L0_ffn_b_proj", [D], F32, kind="ExternalInput").ap()

    A = lambda n, s, d: P.salloc(n, s, d)
    q = QS(P, "p2q")
    wslab = [A(f"p2_wslab{j}", [128, D], F32) for j in range(2)]
    wq_bf = A("p2_wqbf", [128, D], BF16)
    wfcT = A("p2_wfcT", [128, 8 * HID], BF16)
    wprT = A("p2_wprT", [128, 32 * D], BF16)
    bfc_bf = A("p2_bfc", [1, HID], BF16)
    bpr_bf = A("p2_bpr", [1, D], BF16)
    xin = [A(f"p2_xin{j}", [128, D], F32) for j in range(2)]
    xout = [A(f"p2_xout{j}", [128, D], F32) for j in range(2)]
    xn = q.fm8
    xq_bf = A("p2_xqbf", [128, D], BF16)
    xqT = A("p2_xqT", [128, D], BF16)
    hg = A("p2_hg", [128, D], F32)
    hg2 = hg
    hq_bf = A("p2_hqbf", [128, D], BF16)
    hq_bf2 = A("p2_hqbf2", [128, D], BF16)
    hqT = [A("p2_hqT0", [128, 4 * D], BF16)]

    PS = lambda n, s, d: P.palloc(n, s, d)
    b = [PS(f"p2_b{i}", [128, 512], F32) for i in range(6)]
    pT = PS("p2_pT", [128, 1024], BF16)
    pT2 = PS("p2_pT2", [128, 1024], BF16)
    q2 = Ctx()
    q2.y = xout[0]
    q2.g = xout[1]
    q2.i32 = q.i32
    q2.fm8 = q.i32.bitcast(F32)
    q2.am = hg[:, 0:32]
    q2.inv = hg[:, 32:64]
    q2.sc = hg[:, 64:96]
    qs2 = [q, q2]
    wqs = [wq_bf, xq_bf]
    pTs = [pT, pT2]

    eps = cx.cols[:, 0:1]

    P.dma("gpsimd", bfc_bf, bfc_h.rearrange("(o n) -> o n", o=1), stream="weights")
    P.dma("gpsimd", bpr_bf, bpr_h.rearrange("(o n) -> o n", o=1), stream="weights")

    for s in range(32):
        par = s % 2
        sl = wslab[par]
        P.dma("gpsimd", sl[:, 0:D], wfc_h.rearrange("(s p) n -> s p n", p=128)[s],
              stream=f"w{par}")
        quant_chain(P, cx, qs2[par], sl[:, 0:D], wqs[par][:, 0:D])
        for k in range(8):
            P.transpose(pTs[par][:, k * 128:(k + 1) * 128],
                        wqs[par][:, k * 128:(k + 1) * 128], cx.ident)
        dst = wfcT.rearrange("p (k n) -> p k n", k=8)[:, :, 128 * s:128 * s + 128]
        P.v_copy(dst, pTs[par].rearrange("p (k n) -> p k n", k=8))
    for s in range(8):
        for c4 in range(4):
            par = (s * 4 + c4) % 2
            sl = wslab[par]
            P.dma("gpsimd", sl,
                  wpr_h.rearrange("(s p) (c n) -> s p c n", p=128, n=D)[s, :, c4],
                  stream=f"w{par}")
            quant_chain(P, cx, qs2[par], sl, wqs[par][:, 0:D])
            for k in range(8):
                P.transpose(pTs[par][:, k * 128:(k + 1) * 128],
                            wqs[par][:, k * 128:(k + 1) * 128], cx.ident)
            dst = wprT.rearrange("p (k n) -> p k n", k=32)[:, c4 * 8:(c4 + 1) * 8,
                                                          128 * s:128 * s + 128]
            P.v_copy(dst, pTs[par].rearrange("p (k n) -> p k n", k=8))

    rstd_all, negmr_all = ln_prepass(P, nc, cx, x_hbm, n_tiles, "p2pre",
                                     bufs=(xin[0], xin[1], q.g),
                                     stats_in=stats_in)
    x_t = x_hbm.rearrange("(n p) d -> n p d", p=128)
    o_t = out_hbm.rearrange("(n p) d -> n p d", p=128)

    for i in range(n_tiles + 1):
        if i < n_tiles:
            j2 = i % 2
            xi = xin[j2]
            P.dma("sync", xi, x_t[i], stream=f"xin{j2}")
            rstd, negmr = rstd_all[:, i:i + 1], negmr_all[:, i:i + 1]
            P.s_act(xn, xi, AF.Identity, bias=negmr, scale=rstd)
            quant_chain(P, cx, q, xn, xq_bf)
            for k in range(8):
                P.transpose(pT[:, k * 128:(k + 1) * 128],
                            xq_bf[:, k * 128:(k + 1) * 128], cx.ident)
            P.s_copy(xqT, pT)
            hT = hqT[0]
            hgs = [hg, hg2]
            hqs = [hq_bf, hq_bf2]
            for j in range(4):
                pj = (b[0], b[1])
                for half in range(2):
                    col0 = j * D + half * 512
                    pairs = [(xqT[:, k * 128:(k + 1) * 128],
                              wfcT[:, k * HID + col0: k * HID + col0 + 512])
                             for k in range(8)]
                    pairs.append((cx.ones_bf, bfc_bf[0:1, col0:col0 + 512]))
                    P.mm_acc(pj[half], pairs)
                    P.s_act(hgs[j % 2][:, half * 512:(half + 1) * 512], pj[half],
                            AF.Gelu)
                quant_chain(P, cx, q, hgs[j % 2], hqs[j % 2])
                if j >= 1:
                    for k in range(8):
                        P.transpose(pT[:, k * 128:(k + 1) * 128],
                                    hqs[(j - 1) % 2][:, k * 128:(k + 1) * 128],
                                    cx.ident)
                    P.s_copy(hT[:, (j - 1) * D:j * D], pT)
            for k in range(8):
                P.transpose(pT[:, k * 128:(k + 1) * 128],
                            hqs[1][:, k * 128:(k + 1) * 128], cx.ident)
            P.s_copy(hT[:, 3 * D:4 * D], pT)
            yb = (b[2], b[3]) if j2 == 0 else (b[4], b[5])
            for half in range(2):
                pairs = [(hT[:, kk * 128:(kk + 1) * 128],
                          wprT[:, kk * D + half * 512: kk * D + half * 512 + 512])
                         for kk in range(32)]
                pairs.append((cx.ones_bf, bpr_bf[0:1, half * 512:(half + 1) * 512]))
                P.mm_acc(yb[half], pairs)
        if i > 0:
            jp = (i - 1) % 2
            ybp = (b[2], b[3]) if jp == 0 else (b[4], b[5])
            xo = xout[jp]
            for half in range(2):
                sl = slice(half * 512, (half + 1) * 512)
                P.v_tt(xo[:, sl], ybp[half], xin[jp][:, sl], ALU.add)
            P.dma("sync", o_t[i - 1], xo, stream=f"xo{jp}")
            if stats_out is not None:
                emit_out_stats(P, stats_out, xo, xin[jp], i - 1)



def build_pass1(P, nc, cx, x_hbm, out_hbm, n_tiles, stats_out=None):
    H, DK, DV = 8, 128, 128
    win = {}
    for nm in ("Wq", "Wk", "Wv", "Wg", "Wo"):
        h = nc.dram_tensor(f"L0_attn_{nm}", [D, D], F32, kind="ExternalInput").ap()
        win[nm] = load_weight_bf16(P, nc, f"p1_{nm}", h, D, D)

    A = lambda n, s, d: P.salloc(n, s, d)
    xin = [A(f"p1_xin{j}", [128, D], F32) for j in range(2)]
    xout = [A(f"p1_xout{j}", [128, D], F32) for j in range(2)]
    stat = A("p1_stat", [128, 8], F32)
    scratch = A("p1_scratch", [128, D], F32)
    xn_bf = A("p1_xnbf", [128, D], BF16)
    ecn_bf = A("p1_ecnbf", [128, D], BF16)
    xnT = A("p1_xnT", [128, D], BF16)
    qsc = A("p1_qsc", [128, D], F32)
    ksc = A("p1_ksc", [128, D], F32)
    eCn = A("p1_eCn", [128, D], F32)
    eCp = A("p1_eCp", [128, D], F32)
    sp = A("p1_sp", [128, D], F32)
    invq = A("p1_invq", [128, 8], F32)
    invk = A("p1_invk", [128, 8], F32)
    qt_bf = A("p1_qtbf", [128, D], BF16)
    kt_bf = A("p1_ktbf", [128, D], BF16)
    qT = A("p1_qT", [128, D], BF16)
    kT = A("p1_kT", [128, D], BF16)
    k2 = A("p1_k2", [64, 2 * D], BF16)
    v2 = A("p1_v2", [64, 2 * D], BF16)
    am_all = A("p1_amall", [64, 8 * 64], BF16)
    pbs1 = A("p1_pbs", [128, 1024], F32)
    o_sb = A("p1_o", [128, D], BF16)
    oT = [A("p1_oT0", [128, D], BF16), A("p1_oT1", [128, D], BF16)]
    S_f = A("p1_Sf", [128, H * DV], F32)
    S_b = A("p1_Sb", [128, H * DV], BF16)
    Bcol = A("p1_Bcol", [128, 16], F32)

    PS = lambda n, s, d: P.palloc(n, s, d)
    b = [PS(f"p1_b{i}", [128, 512], F32) for i in range(7)]
    pT = PS("p1_pT", [128, 1024], BF16)

    eps = cx.cols[:, 0:1]
    eps2 = cx.cols[:, 1:2]
    one = cx.cols[:, 2:3]

    P.v_memset(S_f, 0.0)
    P.v_memset(S_b, 0.0)

    x_t = x_hbm.rearrange("(n p) d -> n p d", p=128)
    o_t = out_hbm.rearrange("(n p) d -> n p d", p=128)

    for i in range(n_tiles + 1):
      if i < n_tiles:
        xi = xin[i % 2]
        P.dma("sync", xi, x_t[i], stream=f"xin{i % 2}")
        rstd, negmr = ln_stats(P, xi, scratch, stat, eps)
        P.s_act(xn_bf, xi, AF.Identity, bias=negmr, scale=rstd)
        transpose_1024(P, cx, xn_bf, pT, xnT)
        proj_1024(P, (b[0], b[1]), xnT, win["Wq"])
        proj_1024(P, (b[2], b[3]), xnT, win["Wk"])
        for half in range(2):
            P.s_act(scratch[:, half * 512:(half + 1) * 512], b[half], AF.Square)
        P.v_red(invq.rearrange("p (b o) -> p b o", o=1),
                scratch.rearrange("p (b k) -> p b k", k=128), ALU.add)
        for half in range(2):
            P.s_act(scratch[:, half * 512:(half + 1) * 512], b[2 + half], AF.Square)
        P.v_red(invk.rearrange("p (b o) -> p b o", o=1),
                scratch.rearrange("p (b k) -> p b k", k=128), ALU.add)
        for t in (invq, invk):
            P.v_ts(t, t, eps2, ALU.max)
            P.s_act(t, t, AF.Ln)
            P.s_act(t, t, AF.Exp, scale=-0.5)
        proj_1024(P, (b[4], b[5]), xnT, win["Wg"])
        for half in range(2):
            P.s_act(sp[:, half * 512:(half + 1) * 512], b[4 + half], AF.Exp, scale=-1.0)
        P.s_act(sp, sp, AF.Ln, bias=one)
        for half in range(2):
            sl = slice(half * 512, (half + 1) * 512)
            P.mm(b[6], cx.cum64, sp[:, sl])
            P.s_act(eCn[:, sl], b[6], AF.Exp, scale=-1.0)
            P.s_act(eCp[:, sl], b[6], AF.Exp)
        for h in range(H):
            blk = slice(h * 128, (h + 1) * 128)
            P.v_ts(qsc[:, blk], eCn[:, blk], invq[:, h:h + 1], ALU.mult)
            P.v_ts(ksc[:, blk], eCp[:, blk], invk[:, h:h + 1], ALU.mult)
        for half in range(2):
            sl = slice(half * 512, (half + 1) * 512)
            P.v_tt(qt_bf[:, sl], b[half], qsc[:, sl], ALU.mult)
            P.v_tt(kt_bf[:, sl], b[2 + half], ksc[:, sl], ALU.mult)
        proj_1024(P, (b[0], b[1]), xnT, win["Wv"])
        for c in range(2):
            for half in range(2):
                P.s_copy(v2[:, c * D + half * 512: c * D + (half + 1) * 512],
                         b[half][c * 64:(c + 1) * 64, :])
        for c in range(2):
            P.v_copy(k2[:, c * D:(c + 1) * D], kt_bf[c * 64:(c + 1) * 64, :])
        transpose_1024(P, cx, qt_bf, pT, qT)
        transpose_1024(P, cx, kt_bf, pT, kT)
        P.v_copy(ecn_bf, eCn)
        for c in range(2):
            for h in range(H):
                P.mm(b[6][:, c * 8 + h: c * 8 + h + 1],
                     ecn_bf[:, h * 128:(h + 1) * 128], cx.onehot[:, c:c + 1])
        P.v_copy(Bcol, b[6][:, 0:16])
        for c in range(2):
            for h in range(H):
                tcols = slice(h * 128 + c * 64, h * 128 + c * 64 + 64)
                pa = b[4 + h % 2][0:64, (h // 2) * 64:(h // 2) * 64 + 64]
                P.mm(pa, kT[:, tcols], qT[:, tcols])
                P.v_tt(am_all[:, h * 64:(h + 1) * 64], pa, cx.mask_incl64, ALU.mult)
            for h in range(H):
                tcols = slice(h * 128 + c * 64, h * 128 + c * 64 + 64)
                hblk = slice(h * DV, (h + 1) * DV)
                po = b[2 + h // 4][0:64, (h % 4) * 128:(h % 4) * 128 + 128]
                P.mm_acc(po, [(am_all[:, h * 64:(h + 1) * 64],
                               v2[:, c * D + h * 128: c * D + (h + 1) * 128]),
                              (qT[:, tcols], S_b[:, hblk])])
            for h in range(H):
                hblk = slice(h * DV, (h + 1) * DV)
                po = b[2 + h // 4][0:64, (h % 4) * 128:(h % 4) * 128 + 128]
                P.s_copy(o_sb[c * 64:(c + 1) * 64, hblk], po)
            for h in range(H):
                cblk = slice(c * D + h * DV, c * D + (h + 1) * DV)
                pp = b[4 + h // 4][:, (h % 4) * 128:(h % 4) * 128 + 128]
                P.mm(pp, k2[:, cblk], v2[:, cblk])
            bc3 = Bcol[:, c * 8:(c + 1) * 8].rearrange("p (b o) -> p b o", o=1)\
                .broadcast_to([128, 8, 128])
            S3 = S_f.rearrange("p (b k) -> p b k", k=128)
            P.v_tt(S3, S3, bc3, ALU.mult)
            for half in range(2):
                P.v_tt(pbs1.rearrange("p (b k) -> p b k", k=128)[:, half * 4:(half + 1) * 4, :],
                       b[4 + half].rearrange("p (b k) -> p b k", k=128),
                       bc3[:, half * 4:(half + 1) * 4, :], ALU.mult)
            P.v_tt(S_f, S_f, pbs1, ALU.add)
            P.s_copy(S_b, S_f)
        transpose_1024(P, cx, o_sb, pT, oT[i % 2])
      if i > 0:
        jp = (i - 1) % 2
        xo = xout[jp]
        proj_1024(P, (b[0], b[1]), oT[jp], win["Wo"])
        for half in range(2):
            sl = slice(half * 512, (half + 1) * 512)
            P.v_tt(xo[:, sl], b[half], xin[jp][:, sl], ALU.add)
        P.dma("sync", o_t[i - 1], xo, stream=f"xo{jp}")
        if stats_out is not None:
            emit_out_stats(P, stats_out, xo, xin[jp], i - 1)





def build_pass3(P, nc, cx, x_hbm, out_hbm, n_tiles, stats_in=None,
                stats_out=None):
    H, N = 16, 64
    TM, TD = 32, 64
    A = lambda n, s, d: P.salloc(n, s, d)
    dram = lambda n, s: nc.dram_tensor(n, s, F32, kind="ExternalInput").ap()

    win = {}
    for nm in ("Wr", "Wk", "Wv", "Wg", "Wo"):
        win[nm] = load_weight_bf16(P, nc, f"p3_{nm}", dram(f"L1_attn_{nm}", [D, D]), D, D)
    tmw1 = load_weight_bf16(P, nc, "p3_tmw1", dram("L1_attn_tm_w1", [D, 5 * TM]), D, 5 * TM)
    tdw1 = load_weight_bf16(P, nc, "p3_tdw1", dram("L1_attn_td_w1", [D, TD]), D, TD)
    tmw2 = A("p3_tmw2", [TM, 5 * D], BF16)
    P.dma("gpsimd", tmw2.rearrange("d (f c) -> d f c", f=5),
          dram("L1_attn_tm_w2", [5, TM, D]).rearrange("f d c -> d f c"), stream="w0")
    tdw2 = A("p3_tdw2", [TD, D], BF16)
    P.dma("gpsimd", tdw2, dram("L1_attn_td_w2", [TD, D]), stream="w1")

    def rep_row(name, hname, dt):
        t = A(name, [128, D], dt)
        h = dram(hname, [D])
        P.dma("gpsimd" if dt == BF16 else "sync",
              t, h.rearrange("(o n) -> o n", o=1).broadcast_to([128, D]),
              stream="w0" if dt == BF16 else "consts")
        return t

    def plain_row(name, hname):
        t = A(name, [1, D], BF16)
        P.dma("gpsimd", t, dram(hname, [D]).rearrange("(o n) -> o n", o=1), stream="w0")
        return t

    maax = rep_row("p3_maax", "L1_attn_time_maa_x", BF16)
    maar = {}
    for nm in ("w", "k", "v", "r", "g"):
        maar[nm] = plain_row(f"p3_maa{nm}", f"L1_attn_time_maa_{nm}")
    g0 = rep_row("p3_g0", "L1_attn_ln0_g", F32)
    b0r = rep_row("p3_b0r", "L1_attn_ln0_b", F32)
    g1 = rep_row("p3_g1", "L1_attn_ln1_g", BF16)
    b1r = rep_row("p3_b1r", "L1_attn_ln1_b", BF16)
    lnxg = rep_row("p3_lnxg", "L1_attn_lnx_g", BF16)
    lnxb = rep_row("p3_lnxb", "L1_attn_lnx_b", BF16)
    u_rep = A("p3_urep", [128, D], BF16)
    P.dma("gpsimd", u_rep,
          dram("L1_attn_time_faaaa", [H, N]).rearrange("h n -> (h n)")
          .rearrange("(o n) -> o n", o=1).broadcast_to([128, D]), stream="w0")
    tdec = A("p3_tdec", [1, D], BF16)
    P.dma("gpsimd", tdec, dram("L1_attn_time_decay", [D]).rearrange("(o n) -> o n", o=1),
          stream="w1")

    xin = A("p3_xin", [128, D], F32)
    xout = [A(f"p3_xout{j}", [128, D], F32) for j in range(2)]
    xL = [A(f"p3_xL{j}", [128, D], F32) for j in range(2)]
    stat = A("p3_stat", [128, 12], F32)
    scratch = A("p3_scratch", [128, D], F32)
    ruk = scratch
    xa_bf = A("p3_xabf", [128, D], BF16)
    ecn_bf = xa_bf
    carry = A("p3_carry", [1, D], BF16)
    dxa = A("p3_dxa", [128, D], BF16)
    og_bf = A("p3_og", [128, D], BF16)
    shf = og_bf
    xxx = og_bf
    xxxT = A("p3_xxxT", [128, D], BF16)
    xcT = xxxT
    lora_bf = A("p3_lora", [128, 5 * TM], BF16)
    loraT = A("p3_loraT", [TM, 5 * 128], BF16)
    mix_all = A("p3_mixall", [128, 5 * D], BF16)
    kt_bf = mix_all[:, 0 * D:1 * D]
    k_bf = mix_all[:, 1 * D:2 * D]
    v_bf = mix_all[:, 2 * D:3 * D]
    r_bf = mix_all[:, 3 * D:4 * D]
    g_bf = mix_all[:, 4 * D:5 * D]
    t64 = A("p3_t64", [128, TD], BF16)
    t64T = A("p3_t64T", [TD, 128], BF16)
    o_sb = A("p3_o", [128, D], F32)
    ew = o_sb
    edec = A("p3_edec", [128, D], BF16)
    rt_bf = A("p3_rtbf", [128, D], BF16)
    rT = A("p3_rT", [N, H * 128], BF16)
    kT = A("p3_kT", [N, H * 128], BF16)
    Dcol = A("p3_Dcol", [128, H], F32)
    am_all = A("p3_amall", [128, 8 * 128], BF16)
    pbs = A("p3_pbs", [N, 8 * N], F32)
    pb3 = pbs.rearrange("p (b k) -> p b k", k=N)
    ogT = [A(f"p3_ogT{j}", [128, D], BF16) for j in range(2)]
    S_f = A("p3_Sf", [N, H * N], F32)
    S_b = A("p3_Sb", [N, H * N], BF16)
    Bcol = A("p3_Bcol", [N, H], F32)
    gnm = A("p3_gnm", [128, 16], F32)
    gnr = A("p3_gnr", [128, 16], F32)

    PS = lambda n, s, d: P.palloc(n, s, d)
    b = [PS(f"p3_ps{i}", [128, 512], F32) for i in range(6)]
    pTs = [PS("p3_pTa", [128, 1024], BF16), PS("p3_pTb", [128, 1024], BF16)]
    tctr = [0]

    def next_pT():
        tctr[0] += 1
        return pTs[tctr[0] % 2]

    eps = cx.cols[:, 0:1]
    P.v_memset(S_f, 0.0)
    P.v_memset(S_b, 0.0)
    P.v_memset(carry, 0.0)

    x_t = x_hbm.rearrange("(n p) d -> n p d", p=128)
    o_t = out_hbm.rearrange("(n p) d -> n p d", p=128)
    f1_all, f2_all = ln_prepass_double(P, nc, cx, x_hbm, n_tiles, "p3pre",
                                       bufs=(xin, xin, scratch),
                                       stats_in=stats_in)

    for it in range(n_tiles + 2):
        if it >= 2:
            ic = it - 2
            jc = ic % 2
            xo = xout[jc]
            proj_1024(P, (b[0], b[1]), ogT[jc], win["Wo"])
            for half in range(2):
                sl = slice(half * 512, (half + 1) * 512)
                P.v_tt(xo[:, sl], b[half], xL[jc][:, sl], ALU.add)
            P.dma("sync", o_t[ic], xo, stream=f"xo{jc}")
            if stats_out is not None:
                emit_out_stats(P, stats_out, xo, xL[jc], ic)
        if 1 <= it <= n_tiles:
            ib = it - 1
            jb = ib % 2
            for h in range(H):
                P.mm(b[4][0:N, h:h + 1], ecn_bf[:, h * N:(h + 1) * N],
                     cx.onehot[:, 1:2])
            P.v_copy(Bcol, b[4][0:N, 0:H])
            for grp in range(2):
                h0 = grp * 8
                for hh in range(8):
                    h = h0 + hh
                    pa = b[4 + hh % 2]
                    P.mm(pa[:, 0:128], kT[:, h * 128:(h + 1) * 128],
                         rT[:, h * 128:(h + 1) * 128])
                    P.v_tt(am_all[:, hh * 128:(hh + 1) * 128], pa[:, 0:128],
                           cx.mask_strict128, ALU.mult)
                for hh in range(8):
                    h = h0 + hh
                    hn = slice(h * N, (h + 1) * N)
                    P.mm_acc(b[4][:, hh * N:(hh + 1) * N],
                             [(am_all[:, hh * 128:(hh + 1) * 128], v_bf[:, hn]),
                              (rT[:, h * 128:(h + 1) * 128], S_b[:, hn])])
                for hh in range(8):
                    h = h0 + hh
                    hn = slice(h * N, (h + 1) * N)
                    P.v_stt(o_sb[:, hn], v_bf[:, hn], Dcol[:, h:h + 1],
                            b[4][:, hh * N:(hh + 1) * N], ALU.mult, ALU.add)
                for hh in range(8):
                    h = h0 + hh
                    hn = slice(h * N, (h + 1) * N)
                    P.mm(b[5][0:N, hh * N:(hh + 1) * N], kt_bf[:, hn], v_bf[:, hn])
                gsl = slice(h0 * N, (h0 + 8) * N)
                bc3 = Bcol[:, h0:h0 + 8].rearrange("p (b o) -> p b o", o=1)\
                    .broadcast_to([N, 8, N])
                S3 = S_f[:, gsl].rearrange("p (b k) -> p b k", k=N)
                P.v_tt(S3, S3, bc3, ALU.mult)
                P.v_tt(pb3, b[5][0:N, 0:8 * N].rearrange("p (b k) -> p b k", k=N),
                       bc3, ALU.mult)
                P.v_tt(S_f[:, gsl], S_f[:, gsl], pbs, ALU.add)
                P.s_copy(S_b[:, gsl], S_f[:, gsl])
            P.s_act(scratch, o_sb, AF.Square)
            P.v_red(gnm.rearrange("p (b o) -> p b o", o=1),
                    o_sb.rearrange("p (b k) -> p b k", k=64), ALU.add)
            P.v_red(gnr.rearrange("p (b o) -> p b o", o=1),
                    scratch.rearrange("p (b k) -> p b k", k=64), ALU.add)
            P.v_ts(gnm, gnm, 1.0 / 64, ALU.mult)
            P.v_ts(gnr, gnr, 1.0 / 64, ALU.mult)
            P.v_tt(scratch[:, 0:16], gnm, gnm, ALU.mult)
            P.v_tt(gnr, gnr, scratch[:, 0:16], ALU.subtract)
            P.v_ts(gnr, gnr, eps, ALU.add)
            gnrec = scratch[:, 48:64]
            rsqrt_col(P, cx, gnrec, gnr, scratch[:, 16:32], scratch[:, 32:48])
            gm_bc = gnm.rearrange("p (b o) -> p b o", o=1).broadcast_to([128, 16, 64])
            gr_bc = gnrec.rearrange("p (b o) -> p b o", o=1).broadcast_to([128, 16, 64])
            o3 = o_sb.rearrange("p (b k) -> p b k", k=64)
            P.v_tt(o3, o3, gm_bc, ALU.subtract)
            P.v_tt(o3, o3, gr_bc, ALU.mult)
            P.v_tt(o_sb, o_sb, lnxg, ALU.mult)
            P.v_tt(o_sb, o_sb, lnxb, ALU.add)
            P.v_tt(og_bf, o_sb, g_bf, ALU.mult)
            transpose_1024(P, cx, og_bf, next_pT(), ogT[jb])
        if it < n_tiles:
            i = it
            ja = i % 2
            P.dma("sync", xin, x_t[i], stream="xin0")
            P.s_act(scratch, xin, AF.Identity, bias=f2_all[:, i:i + 1],
                    scale=f1_all[:, i:i + 1])
            P.v_tt(xL[ja], scratch, g0, ALU.mult)
            P.v_tt(xL[ja], xL[ja], b0r, ALU.add)
            rstd, negmr = ln_stats_nolut(P, cx, xL[ja], scratch, stat, eps)
            P.s_act(scratch, xL[ja], AF.Identity, bias=negmr, scale=rstd)
            P.v_tt(dxa, scratch, g1, ALU.mult)
            P.v_tt(xa_bf, dxa, b1r, ALU.add)
            P.dma("gpsimd", shf[1:128, :], xa_bf[0:127, :], stream="shift")
            P.dma("gpsimd", shf[0:1, :], carry, stream="shift")
            P.dma("gpsimd", carry, xa_bf[127:128, :], stream="shift")
            P.v_tt(dxa, shf, xa_bf, ALU.subtract)
            P.v_tt(xxx, dxa, maax, ALU.mult)
            P.v_tt(xxx, xxx, xa_bf, ALU.add)
            transpose_1024(P, cx, xxx, next_pT(), xxxT)
            pairs = [(xxxT[:, k * 128:(k + 1) * 128],
                      tmw1[:, k * 5 * TM:(k + 1) * 5 * TM]) for k in range(8)]
            P.mm_acc(b[2][:, 0:5 * TM], pairs)
            P.s_act(lora_bf, b[2][:, 0:5 * TM], AF.Tanh)
            pL = next_pT()
            for c in range(5):
                P.transpose(pL[0:TM, c * 128:(c + 1) * 128],
                            lora_bf[:, c * TM:(c + 1) * TM], cx.ident)
            P.v_copy(loraT, pL[0:TM, 0:5 * 128])
            proj_of = {"r": ("Wr", r_bf, "copy"), "k": ("Wk", k_bf, "copy"),
                       "v": ("Wv", v_bf, "copy"), "g": ("Wg", g_bf, "silu")}
            variants = ("w", "k", "v", "r", "g")
            for ci, cn in enumerate(variants):
                for half in range(2):
                    pr = [(loraT[:, ci * 128:(ci + 1) * 128],
                           tmw2[:, ci * D + half * 512: ci * D + half * 512 + 512]),
                          (cx.ones_bf, maar[cn][0:1, half * 512:(half + 1) * 512])]
                    P.mm_acc(b[ci % 2][:, 0:512] if half == 0 else b[2 + ci % 2][:, 0:512], pr)
                    dst = mix_all[:, ci * D + half * 512: ci * D + (half + 1) * 512]
                    P.s_copy(dst, b[ci % 2] if half == 0 else b[2 + ci % 2])
            m3 = mix_all.rearrange("p (c d) -> p c d", d=D)
            dxa3 = dxa.rearrange("p (o d) -> p o d", o=1).broadcast_to([128, 5, D])
            xa3 = xa_bf.rearrange("p (o d) -> p o d", o=1).broadcast_to([128, 5, D])
            P.v_tt(m3, m3, dxa3, ALU.mult)
            P.v_tt(m3, m3, xa3, ALU.add)
            for ci, cn in enumerate(variants):
                xc_bf = mix_all[:, ci * D:(ci + 1) * D]
                transpose_1024(P, cx, xc_bf, next_pT(), xcT)
                if cn == "w":
                    pr = [(xcT[:, k * 128:(k + 1) * 128],
                           tdw1[:, k * TD:(k + 1) * TD]) for k in range(8)]
                    P.mm_acc(b[2][:, 0:TD], pr)
                    P.s_act(t64, b[2][:, 0:TD], AF.Tanh)
                    pW = next_pT()
                    P.transpose(pW[0:TD, 0:128], t64, cx.ident)
                    P.v_copy(t64T, pW[0:TD, 0:128])
                    for half in range(2):
                        pr = [(t64T[:, 0:128],
                               tdw2[:, half * 512:(half + 1) * 512]),
                              (cx.ones_bf, tdec[0:1, half * 512:(half + 1) * 512])]
                        P.mm_acc(b[half], pr)
                    for half in range(2):
                        P.s_act(ew[:, half * 512:(half + 1) * 512], b[half], AF.Exp)
                else:
                    wname, dst, mode = proj_of[cn]
                    proj_1024(P, (b[2], b[3]), xcT, win[wname])
                    for half in range(2):
                        sl2 = slice(half * 512, (half + 1) * 512)
                        if mode == "silu":
                            P.s_act(dst[:, sl2], b[2 + half], AF.Silu)
                        else:
                            P.s_copy(dst[:, sl2], b[2 + half])
            P.v_tt(ruk[:, 0:D], r_bf, k_bf, ALU.mult)
            P.v_tt(ruk[:, 0:D], ruk[:, 0:D], u_rep, ALU.mult)
            P.v_red(Dcol.rearrange("p (b o) -> p b o", o=1),
                    ruk.rearrange("p (b k) -> p b k", k=64), ALU.add)
            for half in range(2):
                P.mm(b[2 + half], cx.mask_strict128, ew[:, half * 512:(half + 1) * 512])
                P.s_act(edec[:, half * 512:(half + 1) * 512], b[2 + half], AF.Exp,
                        scale=-1.0)
            P.v_tt(rt_bf, r_bf, edec, ALU.mult)
            for half in range(2):
                P.mm(b[2 + half], cx.cum128, ew[:, half * 512:(half + 1) * 512])
                P.s_act(edec[:, half * 512:(half + 1) * 512], b[2 + half], AF.Exp)
            P.v_tt(kt_bf, k_bf, edec, ALU.mult)
            for half in range(2):
                P.s_act(ecn_bf[:, half * 512:(half + 1) * 512], b[2 + half], AF.Exp,
                        scale=-1.0)
            for grp in range(2):
                pR = next_pT()
                for hh in range(8):
                    h = grp * 8 + hh
                    P.transpose(pR[0:N, hh * 128:(hh + 1) * 128],
                                rt_bf[:, h * N:(h + 1) * N], cx.ident)
                P.v_copy(rT[:, grp * 1024:(grp + 1) * 1024], pR[0:N, 0:1024])
                pK = next_pT()
                for hh in range(8):
                    h = grp * 8 + hh
                    P.transpose(pK[0:N, hh * 128:(hh + 1) * 128],
                                kt_bf[:, h * N:(h + 1) * N], cx.ident)
                P.v_copy(kT[:, grp * 1024:(grp + 1) * 1024], pK[0:N, 0:1024])



def build_pass4(P, nc, cx, xmid_hbm, x3_hbm, out_hbm, n_tiles, stats_in=None,
                stats_out=None):
    FF = 3584
    A = lambda n, s, d: P.salloc(n, s, d)
    dram = lambda n, s: nc.dram_tensor(n, s, F32, kind="ExternalInput").ap()
    wk = load_weight_bf16(P, nc, "p4_wk", dram("L1_attn_Wk_ffn", [D, FF]), D, FF)
    wv = load_weight_bf16(P, nc, "p4_wv", dram("L1_attn_Wv_ffn", [FF, D]), FF, D)
    wr = load_weight_bf16(P, nc, "p4_wr", dram("L1_attn_Wr_ffn", [D, D]), D, D)
    g2 = A("p4_g2", [128, D], F32)
    P.dma("sync", g2, dram("L1_attn_ln2_g", [D]).rearrange("(o n) -> o n", o=1)
          .broadcast_to([128, D]), stream="consts")
    b2 = A("p4_b2", [128, D], F32)
    P.dma("sync", b2, dram("L1_attn_ln2_b", [D]).rearrange("(o n) -> o n", o=1)
          .broadcast_to([128, D]), stream="consts")
    maak = A("p4_maak", [128, D], BF16)
    P.dma("gpsimd", maak, dram("L1_attn_ffn_maa_k", [D]).rearrange("(o n) -> o n", o=1)
          .broadcast_to([128, D]), stream="w0")
    maar = A("p4_maar", [128, D], BF16)
    P.dma("gpsimd", maar, dram("L1_attn_ffn_maa_r", [D]).rearrange("(o n) -> o n", o=1)
          .broadcast_to([128, D]), stream="w0")

    xin = [A(f"p4_xin{j}", [128, D], F32) for j in range(2)]
    x3in = [A(f"p4_x3in{j}", [128, D], F32) for j in range(2)]
    xout = [A(f"p4_xout{j}", [128, D], F32) for j in range(2)]
    scratch = A("p4_scratch", [128, D], F32)
    relu_s = A("p4_relus", [128, D], BF16)
    xf_bf = A("p4_xfbf", [128, D], BF16)
    shf = A("p4_shf", [128, D], BF16)
    carry = A("p4_carry", [1, D], BF16)
    dxf = A("p4_dxf", [128, D], BF16)
    xv_bf = shf
    xvTk = [A(f"p4_xvTk{j}", [128, D], BF16) for j in range(2)]
    xvTr = [A(f"p4_xvTr{j}", [128, D], BF16) for j in range(2)]
    kk_bf = A("p4_kk", [128, D], BF16)
    kkT = A("p4_kkT", [128, FF], BF16)
    sig = None
    sigs = [A("p4_sig0", [128, D], F32), A("p4_sig1", [128, D], F32)]

    PS = lambda n, s, d: P.palloc(n, s, d)
    b = [PS(f"p4_ps{i}", [128, 512], F32) for i in range(6)]
    pT = PS("p4_pT", [128, 1024], BF16)
    pT2 = PS("p4_pT2", [128, 1024], BF16)

    eps = cx.cols[:, 0:1]
    P.v_memset(carry, 0.0)

    xm_t = xmid_hbm.rearrange("(n p) d -> n p d", p=128)
    x3_t = x3_hbm.rearrange("(n p) d -> n p d", p=128)
    o_t = out_hbm.rearrange("(n p) d -> n p d", p=128)

    rstd_all, negmr_all = ln_prepass(P, nc, cx, xmid_hbm, n_tiles, "p4pre",
                                     bufs=(xin[0], xin[1], scratch),
                                     stats_in=stats_in)
    for it in range(n_tiles + 2):
      if it >= 2:
        jp = (it - 2) % 2
        ybp = (b[0], b[1]) if jp == 0 else (b[2], b[3])
        xo = xout[jp]
        for half in range(2):
            sl = slice(half * 512, (half + 1) * 512)
            P.v_tt(xo[:, sl], ybp[half], sigs[jp][:, sl], ALU.mult)
            P.v_tt(xo[:, sl], xo[:, sl], xin[jp][:, sl], ALU.add)
            P.v_tt(xo[:, sl], xo[:, sl], x3in[jp][:, sl], ALU.add)
        P.dma("sync", o_t[it - 2], xo, stream=f"xo{jp}")
        if stats_out is not None:
            emit_out_stats(P, stats_out, xo, x3in[jp], it - 2)
      if it < n_tiles:
        j2 = it % 2
        xi = xin[j2]
        x3 = x3in[j2]
        P.dma("sync", xi, xm_t[it], stream=f"xin{j2}")
        P.dma("sync", x3, x3_t[it], stream=f"x3in{j2}")
        rstd, negmr = rstd_all[:, it:it + 1], negmr_all[:, it:it + 1]
        P.s_act(scratch, xi, AF.Identity, bias=negmr, scale=rstd)
        P.v_tt(dxf, scratch, g2, ALU.mult)
        P.v_tt(xf_bf, dxf, b2, ALU.add)
        P.dma("gpsimd", shf[1:128, :], xf_bf[0:127, :], stream="shift")
        P.dma("gpsimd", shf[0:1, :], carry, stream="shift")
        P.dma("gpsimd", carry, xf_bf[127:128, :], stream="shift")
        P.v_tt(dxf, shf, xf_bf, ALU.subtract)
        P.v_tt(xv_bf, dxf, maak, ALU.mult)
        P.v_tt(xv_bf, xv_bf, xf_bf, ALU.add)
        transpose_1024(P, cx, xv_bf, pT2, xvTk[j2])
        P.v_tt(xv_bf, dxf, maar, ALU.mult)
        P.v_tt(xv_bf, xv_bf, xf_bf, ALU.add)
        transpose_1024(P, cx, xv_bf, pT2, xvTr[j2])
      if 1 <= it <= n_tiles:
        im = it - 1
        jm = im % 2
        xvT = xvTk[jm]
        for jc in range(4):
            w0 = jc * 1024
            wid = min(FF - w0, 1024)
            nh = (wid + 511) // 512
            pj = (b[0], b[1]) if jm == 0 else (b[2], b[3])
            for half in range(nh):
                c0 = w0 + half * 512
                c1 = min(FF, c0 + 512)
                pairs = [(xvT[:, k * 128:(k + 1) * 128],
                          wk[:, k * FF + c0: k * FF + c1]) for k in range(8)]
                P.mm_acc(pj[half][:, 0:c1 - c0], pairs)
                P.s_act(relu_s[:, half * 512:half * 512 + c1 - c0],
                        pj[half][:, 0:c1 - c0], AF.Relu)
                P.s_act(kk_bf[:, c0 - w0:c1 - w0],
                        relu_s[:, half * 512:half * 512 + c1 - c0], AF.Square)
            for k in range(8 if wid == 1024 else 4):
                if w0 + k * 128 < FF:
                    P.transpose(pT[:, k * 128:(k + 1) * 128],
                                kk_bf[:, k * 128:(k + 1) * 128], cx.ident)
            P.s_copy(kkT[:, w0:w0 + wid], pT[:, 0:wid])
        proj_1024(P, (b[4], b[5]), xvTr[jm], wr)
        sig = sigs[jm]
        for half in range(2):
            P.s_act(sig[:, half * 512:(half + 1) * 512], b[4 + half], AF.Sigmoid)
        yb = (b[0], b[1]) if jm == 0 else (b[2], b[3])
        for half in range(2):
            pairs = [(kkT[:, kk2 * 128:(kk2 + 1) * 128],
                      wv[:, kk2 * D + half * 512: kk2 * D + half * 512 + 512])
                     for kk2 in range(28)]
            P.mm_acc(yb[half], pairs)





def round_floor_chain(P, cx, q, y_ap, out_bf, W, lo, hi):
    half = cx.cols[:, 3:4]
    neghalf = cx.cols[:, 4:5]
    P.s_act(q.i32[:, 0:W], y_ap, AF.Identity, bias=half)
    P.s_act(q.fm8[:, 0:W], q.i32[:, 0:W], AF.Identity, scale=1.0, bias=cx.cols[:, 6:7])
    P.v_tt(q.g[:, 0:W], q.fm8[:, 0:W], y_ap, ALU.is_gt)
    P.v_stt(q.y[:, 0:W], q.fm8[:, 0:W], 0.5, q.g[:, 0:W], ALU.add, ALU.subtract)
    P.v_ts(out_bf, q.y[:, 0:W], float(lo), ALU.max, float(hi), ALU.min)


def build_pass5(P, nc, cx, x_hbm, out_hbm, n_tiles, stats_in=None):
    HID = 4096
    ST = 256
    NS = n_tiles // 2
    A = lambda n, s, d: P.salloc(n, s, d)
    dram = lambda n, s: nc.dram_tensor(n, s, F32, kind="ExternalInput").ap()
    w1_h = dram("L1_ffn_w1", [HID, D])
    w2_h = dram("L1_ffn_w2", [D, HID])
    cw_h = dram("L1_ffn_conv_w", [HID, 1, 3])
    cb_h = dram("L1_ffn_conv_b", [HID])

    q = QS(P, "p5q")
    T1T = A("p5_T1T", [128, 8 * HID], BF16)
    T2T = A("p5_T2T", [128, 32 * D], BF16)
    _wsl = A("p5_wsl0", [128, D], F32)
    cwall = A("p5_cw", [128, 96], BF16)
    cball = A("p5_cb", [128, 32], BF16)
    acc = A("p5_acc", [128, 2], F32)
    swrow = A("p5_swrow", [1, 2], F32)
    mcols = A("p5_mcols", [128, 2], F32)
    minv = A("p5_minv", [128, 2], F32)
    onesf = A("p5_onesf", [1, 128], F32)
    _xi = A("p5_xin0", [128, D], F32)
    xin = [_xi, _xi]
    xout = [A(f"p5_xout{j}", [128, D], F32) for j in range(2)]
    wslab = [_wsl, xout[0]]
    stat = A("p5_stat", [128, 12], F32)
    scratch = q.y
    xn = q.fm8
    nx_bf = A("p5_nx", [128, D], BF16)
    wq_bf = nx_bf
    nxT = A("p5_nxT0", [128, 8 * ST], BF16)
    xcin = [q.i32.bitcast(F32), q.i32.bitcast(F32)]
    ivc = A("p5_ivc", [128, 2], F32)
    invs2 = A("p5_invs2", [128, 2], F32)
    h_bf = [A(f"p5_h{j}", [128, 32 * ST], BF16) for j in range(2)]
    hlast = A("p5_hlast", [128, 32], BF16)
    hfirst = A("p5_hfirst", [128, 32], BF16)
    qc = Ctx()
    qc.y = A("p5_qcy", [128, D], F32)
    qc.g = A("p5_qcg", [128, D], F32)
    qc.i32 = q.i32
    qc.fm8 = q.fm8
    tmpb = qc.g.bitcast(BF16)
    tmp2b = qc.y.bitcast(BF16)
    mt2 = _wsl[:, 0:ST]
    srep = _wsl[:, ST:2 * ST]
    rowtmp = _wsl[0:1, 2 * ST:3 * ST]
    invrep = _wsl[:, 3 * ST:4 * ST]

    PS = lambda n, s, d: P.palloc(n, s, d)
    b = [PS(f"p5_ps{i}", [128, 512], F32) for i in range(6)]
    pT = PS("p5_pT", [128, 1024], BF16)
    pT2 = PS("p5_pT2", [128, 1024], BF16)
    pTs = [pT, pT2]
    wqs = [nx_bf, tmpb[:, 0:D]]
    qs2 = [q, qc]

    eps = cx.cols[:, 0:1]
    P.v_memset(onesf, 1.0)
    P.v_memset(hfirst, 0.0)
    P.v_memset(hlast, 0.0)

    for wi, (wh, rows) in enumerate(((w1_h, HID), (w2_h, D))):
        nslab = rows // 128
        ac = acc[:, wi:wi + 1]
        P.v_memset(ac, 0.0)
        for s in range(nslab):
            for c4 in range(4 if wh is w2_h else 1):
                cidx = s * 4 + c4 if wh is w2_h else s
                sl = wslab[cidx % 2]
                if wh is w1_h:
                    P.dma("gpsimd", sl, wh.rearrange("(s p) n -> s p n", p=128)[s],
                          stream=f"w{cidx % 2}")
                else:
                    P.dma("gpsimd", sl,
                          wh.rearrange("(s p) (c n) -> s p c n", p=128, n=D)[s, :, c4],
                          stream=f"w{cidx % 2}")
                P.v_red(stat[:, cidx % 2:cidx % 2 + 1], sl, ALU.add, absval=True)
                P.v_tt(ac, ac, stat[:, cidx % 2:cidx % 2 + 1], ALU.add)
        P.mm(b[0][0:1, 0:1], ac, cx.cols[:, 2:3])
        P.v_ts(swrow[0:1, wi:wi + 1], b[0][0:1, 0:1], 1.0 / (HID * D), ALU.mult)
        P.v_ts(swrow[0:1, wi:wi + 1], swrow[0:1, wi:wi + 1], 1e-5, ALU.max)
        P.mm(b[1][:, 0:1], onesf, swrow[0:1, wi:wi + 1])
        P.v_copy(mcols[:, wi:wi + 1], b[1][:, 0:1])
    P.v_recip(minv, mcols)

    for wi, (wh, rows, dstT) in enumerate(((w1_h, HID, T1T), (w2_h, D, T2T))):
        nslab = rows // 128
        sw = minv[:, wi:wi + 1]
        for s in range(nslab):
            nchunk = 1 if wh is w1_h else 4
            for c4 in range(nchunk):
                cidx = s * nchunk + c4
                par = cidx % 2
                sl = wslab[par]
                qq = qs2[par]
                wq2 = wqs[par]
                pTt = pTs[par]
                if wh is w1_h:
                    P.dma("gpsimd", sl, wh.rearrange("(s p) n -> s p n", p=128)[s],
                          stream=f"w{par}")
                else:
                    P.dma("gpsimd", sl,
                          wh.rearrange("(s p) (c n) -> s p c n", p=128, n=D)[s, :, c4],
                          stream=f"w{par}")
                P.v_ts(qq.y[:, 0:D], sl, sw, ALU.mult)
                round_floor_chain(P, cx, qq, qq.y[:, 0:D], wq2, D, -1, 1)
                for k in range(8):
                    P.transpose(pTt[:, k * 128:(k + 1) * 128],
                                wq2[:, k * 128:(k + 1) * 128], cx.ident)
                if wh is w1_h:
                    dst = dstT.rearrange("p (k n) -> p k n", k=8)[:, :, 128 * s:128 * s + 128]
                else:
                    dst = dstT.rearrange("p (k n) -> p k n", k=32)[:, c4 * 8:(c4 + 1) * 8,
                                                                  128 * s:128 * s + 128]
                P.v_copy(dst, pTt.rearrange("p (k n) -> p k n", k=8))
    P.dma("gpsimd", cwall.rearrange("p (o s t) -> p o s t", o=1, s=32),
          cw_h.rearrange("(s p) o t -> p o s t", p=128), stream="w0",
          allow_slow_non_contiguous=True)
    P.dma("gpsimd", cball, cb_h.rearrange("(s p) -> p s", p=128), stream="w1",
          allow_slow_non_contiguous=True)

    rstd_all, negmr_all = ln_prepass(P, nc, cx, x_hbm, n_tiles, "p5pre",
                                     bufs=(xin[0], xin[1], q.g),
                                     stats_in=stats_in)
    x_t = x_hbm.rearrange("(n p) d -> n p d", p=128)
    o_t = out_hbm.rearrange("(n p) d -> n p d", p=128)
    cw3 = cwall.rearrange("p (s t) -> p s t", s=32)
    h3view = lambda hb: hb.rearrange("p (s t) -> p s t", t=ST)

    for s in range(NS + 1):
        js = s % 2
        if s < NS:
            for t2 in range(2):
                i = 2 * s + t2
                xi = xin[t2]
                P.dma("sync", xi, x_t[i], stream=f"xin{t2}")
                rstd, negmr = rstd_all[:, i:i + 1], negmr_all[:, i:i + 1]
                P.s_act(xn, xi, AF.Identity, bias=negmr, scale=rstd)
                mx, mxc, sc, iv = (stat[:, t:t + 1] for t in (5, 6, 7, 8))
                P.v_red(mx, xn, ALU.max, absval=True)
                P.v_ts(mxc, mx, 1e-5, ALU.max)
                P.v_recip(sc, mxc)
                P.v_ts(sc, sc, 127.0, ALU.mult)
                P.v_ts(q.y[:, 0:D], xn, sc, ALU.mult)
                round_floor_chain(P, cx, q, q.y[:, 0:D], nx_bf, D, -128, 127)
                P.v_tt(ivc[:, t2:t2 + 1], mxc, mcols[:, 0:1], ALU.mult)
                pTt = pT
                for k in range(8):
                    P.transpose(pTt[:, k * 128:(k + 1) * 128],
                                nx_bf[:, k * 128:(k + 1) * 128], cx.ident)
                P.v_copy(nxT.rearrange("p (k n) -> p k n", k=8)[:, :, t2 * 128:(t2 + 1) * 128],
                         pTt.rearrange("p (k n) -> p k n", k=8))
            P.v_ts(ivc, ivc, 1.0 / 127.0, ALU.mult)
            for t2 in range(2):
                P.mm(b[4][0:1, t2 * 128:(t2 + 1) * 128], ivc[:, t2:t2 + 1], cx.eyef)
            P.v_copy(rowtmp, b[4][0:1, 0:ST])
            P.mm(b[4][:, 0:ST], onesf, rowtmp)
            P.v_copy(invrep, b[4][:, 0:ST])
            hb = h_bf[js]
            for blk in range(32):
                bank = b[blk % 2]
                pairs = [(T1T[:, k * HID + blk * 128: k * HID + blk * 128 + 128],
                          nxT[:, k * ST:(k + 1) * ST]) for k in range(8)]
                P.mm_acc(bank[:, 0:ST], pairs)
                iv3 = invrep.rearrange("p (o t) -> p o t", o=1)
                P.v_tt(hb.rearrange("p (s t) -> p s t", t=ST)[:, blk:blk + 1, :],
                       bank[:, 0:ST].rearrange("p (o t) -> p o t", o=1), iv3, ALU.mult)
            P.v_copy(hfirst, h3view(hb)[:, :, 0:1].rearrange("p s o -> p (s o)"))
        if s == NS:
            P.v_memset(hfirst, 0.0)
        if s > 0:
            jp = (s - 1) % 2
            hp = h3view(h_bf[jp])
            ys = h_bf[jp]
            nh_bf = h_bf[jp]
            P.v_copy(hlast, hp[:, :, ST - 1:ST].rearrange("p s o -> p (s o)"))
            for c4 in range(4):
                sv = slice(c4 * 8, (c4 + 1) * 8)
                hpc = hp[:, sv, :]
                tp = tmpb.rearrange("p (s t) -> p s t", t=ST)[:, 0:8, :]
                tq = tmp2b.rearrange("p (s t) -> p s t", t=ST)[:, 0:8, :]
                cw0b = cw3[:, sv, 0:1].broadcast_to([128, 8, ST])
                cw1b = cw3[:, sv, 1:2].broadcast_to([128, 8, ST])
                cw2b = cw3[:, sv, 2:3].broadcast_to([128, 8, ST])
                cbb = cball[:, sv].rearrange("p (s o) -> p s o", o=1).broadcast_to([128, 8, ST])
                P.v_tt(tp[:, :, 0:ST - 1], hpc[:, :, 0:ST - 1], cw0b[:, :, 0:ST - 1], ALU.mult)
                P.v_tt(tq[:, :, 0:ST - 1], hpc[:, :, 1:ST], cw2b[:, :, 1:ST], ALU.mult)
                P.v_tt(hpc, hpc, cw1b, ALU.mult)
                P.v_tt(hpc, hpc, cbb, ALU.add)
                P.v_tt(hpc[:, :, 1:ST], hpc[:, :, 1:ST], tp[:, :, 0:ST - 1], ALU.add)
                P.v_tt(hpc[:, :, 0:ST - 1], hpc[:, :, 0:ST - 1], tq[:, :, 0:ST - 1], ALU.add)
            hl3 = hlast.rearrange("p (s o) -> p s o", o=1)
            hf3 = hfirst.rearrange("p (s o) -> p s o", o=1)
            tpz = tmpb[:, 0:32].rearrange("p (s o) -> p s o", o=1)
            P.v_tt(tpz, hl3, cw3[:, :, 0:1], ALU.mult)
            P.v_tt(hp[:, :, 0:1], hp[:, :, 0:1], tpz, ALU.add)
            P.v_tt(tpz, hf3, cw3[:, :, 2:3], ALU.mult)
            P.v_tt(hp[:, :, ST - 1:ST], hp[:, :, ST - 1:ST], tpz, ALU.add)
            P.s_act(ys, ys, AF.Silu)
            h3y = h3view(ys)
            tr0 = tmpb.rearrange("p (s t) -> p s t", t=ST)
            tr1 = tmp2b.rearrange("p (s t) -> p s t", t=ST)
            U16 = mybir.dt.uint16
            P.v_ts(tmpb.bitcast(U16), ys[:, 0:8 * ST].bitcast(U16), 0x7FFF,
                   ALU.bitwise_and)
            P.v_ts(tmp2b.bitcast(U16), ys[:, 8 * ST:16 * ST].bitcast(U16), 0x7FFF,
                   ALU.bitwise_and)
            P.v_tt(tr0, tr0, tr1, ALU.max)
            P.v_ts(tmp2b.bitcast(U16), ys[:, 16 * ST:24 * ST].bitcast(U16), 0x7FFF,
                   ALU.bitwise_and)
            P.v_tt(tr0, tr0, tr1, ALU.max)
            P.v_ts(tmp2b.bitcast(U16), ys[:, 24 * ST:32 * ST].bitcast(U16), 0x7FFF,
                   ALU.bitwise_and)
            P.v_tt(tr0, tr0, tr1, ALU.max)
            P.v_tt(tr0[:, 0:4, :], tr0[:, 0:4, :], tr0[:, 4:8, :], ALU.max)
            P.v_tt(tr0[:, 0:2, :], tr0[:, 0:2, :], tr0[:, 2:4, :], ALU.max)
            P.v_tt(nx_bf[:, 0:ST].rearrange("p (o t) -> p o t", o=1),
                   tr0[:, 0:1, :], tr0[:, 1:2, :], ALU.max)
            for t2 in range(2):
                P.transpose(pT[:, t2 * 128:(t2 + 1) * 128],
                            nx_bf[:, t2 * 128:(t2 + 1) * 128], cx.ident)
            P.v_copy(mt2, pT[:, 0:ST])
            mxc2 = stat[:, 9:11]
            P.v_red(mxc2.rearrange("p (t o) -> p t o", o=1),
                    mt2.rearrange("p (t l) -> p t l", l=128), ALU.max)
            P.v_ts(mxc2, mxc2, 1e-5, ALU.max)
            P.v_recip(invs2, mxc2)
            P.v_ts(invs2, invs2, 127.0, ALU.mult)
            for t2 in range(2):
                P.mm(b[4][0:1, t2 * 128:(t2 + 1) * 128], invs2[:, t2:t2 + 1], cx.eyef)
            P.v_copy(rowtmp, b[4][0:1, 0:ST])
            P.mm(b[4][:, 0:ST], onesf, rowtmp)
            P.v_copy(srep, b[4][:, 0:ST])
            P.v_tt(invs2, mxc2, mcols[:, 1:2].broadcast_to([128, 2]), ALU.mult)
            P.v_ts(invs2, invs2, 1.0 / 127.0, ALU.mult)
            for c4 in range(8):
                sl4 = slice(c4 * D, (c4 + 1) * D)
                sr3 = srep.rearrange("p (o t) -> p o t", o=1).broadcast_to([128, 4, ST])
                P.v_tt(qc.y[:, 0:D].rearrange("p (s t) -> p s t", t=ST),
                       ys[:, sl4].rearrange("p (s t) -> p s t", t=ST), sr3, ALU.mult)
                round_floor_chain(P, cx, qc, qc.y[:, 0:D], nh_bf[:, sl4], D, -128, 127)
            nh3 = hp
            for t2 in range(2):
                i = 2 * (s - 1) + t2
                yb = (b[2], b[3]) if t2 == 0 else (b[4], b[5])
                for half in range(2):
                    pairs = [(nh3[:, sl5, t2 * 128:(t2 + 1) * 128],
                              T2T[:, sl5 * D + half * 512: sl5 * D + half * 512 + 512])
                             for sl5 in range(32)]
                    P.mm_acc(yb[half], pairs)
                xc = xcin[t2]
                P.dma("sync", xc, x_t[i], stream=f"xc{t2}")
                xo = xout[t2]
                for half in range(2):
                    sl6 = slice(half * 512, (half + 1) * 512)
                    P.v_stt(xo[:, sl6], yb[half], invs2[:, t2:t2 + 1], xc[:, sl6],
                            ALU.mult, ALU.add)
                P.dma("sync", o_t[i], xo, stream=f"xo{t2}")



def build_nc(passes, n_tiles=SEQ // 128, p4_x3_external=False):
    nc = bass.Bass("TRN2", target_bir_lowering=False)
    P = Prog(nc)
    cx = setup_consts(P, nc)
    x_in = nc.dram_tensor("L0_attn_x", [n_tiles * 128, D], F32, kind="ExternalInput").ap()
    bufs = {"x": x_in}
    for nm in ("x1", "x2", "xm", "x4"):
        bufs[nm] = nc.dram_tensor("stage_" + nm, [n_tiles * 128, D], F32, kind="Internal").ap()
    out = nc.dram_tensor("out", [n_tiles * 128, D], F32, kind="ExternalOutput").ap()
    bufs["out"] = out
    io = {"p1": ("x", "x1"), "p2": ("x1", "x2"), "p3": ("x2", "xm"),
          "p4": ("xm", "x4"), "p5": ("x4", "out")}
    last = passes[-1]
    builders = {"p1": build_pass1, "p2": build_pass2, "p3": build_pass3,
                "p4": build_pass4, "p5": build_pass5}
    chain_stats = {}
    shared_pairs = []
    for a, bname in zip(passes, passes[1:]):
        if io[a][1] == io[bname][0]:
            k = len(chain_stats) % 2
            if k >= len(shared_pairs):
                shared_pairs.append(
                    (P.salloc(f"hs{k}_sums", [128, n_tiles], F32),
                     P.salloc(f"hs{k}_ssq", [128, n_tiles], F32)))
            chain_stats[bname] = shared_pairs[k]
    for idx, p in enumerate(passes):
        src, dst = io[p]
        if p == passes[0]:
            src = "x"
        if p == last:
            dst = "out"
        kw = {}
        if p in chain_stats:
            kw["stats_in"] = chain_stats[p]
        if idx + 1 < len(passes) and passes[idx + 1] in chain_stats:
            kw["stats_out"] = chain_stats[passes[idx + 1]]
        P.guards = []
        if p == "p4":
            if p4_x3_external:
                x3b = nc.dram_tensor("x3_in", [n_tiles * 128, D], F32,
                                     kind="ExternalInput").ap()
            elif passes[0] == "p4":
                x3b = bufs["x"]
            else:
                x3b = bufs["x2"]
            builders[p](P, nc, cx, bufs[src], x3b, bufs[dst], n_tiles, **kw)
        else:
            builders[p](P, nc, cx, bufs[src], bufs[dst], n_tiles, **kw)
        P.barrier()
        for g in reversed(P.guards):
            g.__exit__(None, None, None)
        P.guards = None
    P.final_waits()
    P.finalize()
    return nc


def _declared_inputs(nc):
    names = []
    for a in nc.m.functions[0].allocations:
        if isinstance(a, mybir.MemoryLocationSet) and a.kind == "ExternalInput":
            names.append(a.memorylocations[0].name)
    return set(names)


_NC_CACHE = {}


def _launch(passes, per_core_x, all_inputs, consts, extra=None):
    B = len(per_core_x)
    T = per_core_x[0].shape[0]
    key = (tuple(passes), T, extra is not None)
    if key not in _NC_CACHE:
        _NC_CACHE[key] = build_nc(list(passes), n_tiles=T // 128,
                                  p4_x3_external=extra is not None)
    nc = _NC_CACHE[key]
    declared = _declared_inputs(nc)
    in_maps = []
    for c in range(B):
        m = {}
        for nm in declared:
            if nm == "partition_id":
                continue
            if nm == "L0_attn_x":
                m[nm] = per_core_x[c]
            elif nm == "x3_in":
                m[nm] = extra[c]
            elif nm in consts:
                m[nm] = consts[nm]
            else:
                m[nm] = np.ascontiguousarray(np.asarray(all_inputs[nm], np.float32))
        in_maps.append(m)
    res = run_bass_kernel_spmd(nc, in_maps, core_ids=list(range(B)))
    return [r["out"] for r in res.results]


def kernel(**inputs):
    x = np.ascontiguousarray(np.asarray(inputs["L0_attn_x"], dtype=np.float32))
    B, T, d = x.shape
    consts = const_arrays()
    cur = [x[c] for c in range(B)]
    out = _launch(["p1", "p2", "p3", "p4", "p5"], cur, inputs, consts)
    return np.stack(out, axis=0)
